# version 20 (final)
# baseline (speedup 1.0000x reference)
import math
from contextlib import ExitStack

import numpy as np
import concourse.bass as bass
import concourse.mybir as mybir
from concourse.bass_utils import run_bass_kernel_spmd

F32 = mybir.dt.float32
BF16 = mybir.dt.bfloat16
AF = mybir.ActivationFunctionType
ALU = mybir.AluOpType

NCORES = 8
B, T, D = 16, 2048, 1024
SEQ = B // NCORES
NC = D // 128
NTT = T // 512
EPS = 1e-5

H1 = 4
DK1 = 256
DV1 = 512
GAMMAS = [1.0 - 2.0 ** (-5.0 - h) for h in range(H1)]
CCH = 512
NCH = T // CCH
NEXP = 4
H2 = 8
MS = [D // (2 ** e) for e in range(NEXP)]
MCH = [m // 128 for m in MS]
MASK0_DELTAS = [-128, 0, 128, 256, 384, 512]

ENGS = ("sync", "vector", "scalar", "tensor", "gpsimd")



class Site:

    def __init__(self, prog, name):
        self.prog = prog
        self.key = ("s", name)
        self.count = 0
        self.gate = None

    def dma(self, eng, out_ap, in_ap, wait=(), first=False, transpose=False):
        if first and self.gate:
            g = self.gate
            if (isinstance(g, tuple) and len(g) == 2 and
                    isinstance(g[0], tuple) and isinstance(g[1], int)):
                g = (g,)
            wait = tuple(wait) + tuple(g)
        return self.prog._dma(eng, out_ap, in_ap, self, wait, transpose)

    def tok(self):
        return (self.key, self.count * 16)


class Prog:
    def __init__(self, nc):
        self.nc = nc
        self.items = {e: [] for e in ENGS}
        self.ecnt = {e: 0 for e in ENGS}
        self.waited = {e: {} for e in ENGS}
        self.sites = {}

    def site(self, name):
        if name not in self.sites:
            self.sites[name] = Site(self, name)
        return self.sites[name]

    def comp(self, eng, builder, wait=(), inc=False, drain=True):
        waits = self._norm(eng, wait)
        tok = None
        if inc:
            self.ecnt[eng] += 1
            tok = (("e", eng), self.ecnt[eng])
        self.items[eng].append(
            (builder, waits, tok, False,
             drain and eng in ("vector", "scalar", "gpsimd")))
        return tok

    def _dma(self, eng, out_ap, in_ap, site, wait, transpose):
        waits = self._norm(eng, wait)
        site.count += 1
        tok = (site.key, site.count * 16)
        if transpose:
            def builder(e, o=out_ap, i=in_ap):
                return e.dma_start_transpose(o, i)
        else:
            def builder(e, o=out_ap, i=in_ap):
                return e.dma_start(out=o, in_=i)
        self.items[eng].append((builder, waits, tok, True, False))
        return tok

    def _norm(self, eng, wait):
        out = []
        for t in wait:
            if t is None:
                continue
            key, val = t
            if key[0] == "e" and key[1] == eng:
                continue
            if self.waited[eng].get(key, -1) >= val:
                continue
            self.waited[eng][key] = val
            out.append((key, val))
        return out

    def replay(self):
        nc = self.nc
        es = ExitStack()
        sems = {}
        for e in ENGS:
            sems[("e", e)] = es.enter_context(nc.semaphore(f"se_{e}"))
        for name, s in self.sites.items():
            sems[s.key] = es.enter_context(nc.semaphore(f"ss_{name}"))
        with es:
            with nc.Block() as block:
                def make(name):
                    def run(eng):
                        for builder, waits, tok, is_dma, drain in self.items[name]:
                            for key, val in waits:
                                eng.wait_ge(sems[key], val)
                            if builder is None:
                                continue
                            ins = builder(eng)
                            if tok is not None:
                                ins.then_inc(sems[tok[0]], 16 if is_dma else 1)
                            if drain:
                                eng.drain()
                    return run
                block.sync(make("sync"))
                block.vector(make("vector"))
                block.scalar(make("scalar"))
                block.tensor(make("tensor"))
                block.gpsimd(make("gpsimd"))


class Arena:

    def __init__(self, nc, cols):
        self.base = nc.alloc_sbuf_tensor("arena", [128, cols], BF16).ap()
        self.cols = cols
        self.off = 0
        self.peak = 0
        self.stack = []

    def push(self):
        self.stack.append(self.off)

    def pop(self):
        self.off = self.stack.pop()

    def bf16(self, n):
        o = self._take(n)
        return self.base[:, o:o + n]

    def f32(self, n):
        o = self._take(2 * n, 2)
        return self.base[:, o:o + 2 * n].bitcast(F32)

    def row(self, n):
        o = self._take(2 * n, 2)
        return self.base[0:1, o:o + 2 * n].bitcast(F32)

    def _take(self, n, align=1):
        if self.off % align:
            self.off += align - self.off % align
        o = self.off
        self.off += n
        self.peak = max(self.peak, self.off)
        assert self.off <= self.cols, f"arena overflow {self.off}>{self.cols}"
        return o



def _host_constants():
    import ml_dtypes
    c = {}
    p = np.arange(128)[:, None]
    f = np.arange(512)[None, :]
    m0 = np.zeros((len(MASK0_DELTAS), 128, 512), np.float32)
    for i, d in enumerate(MASK0_DELTAS):
        m0[i] = (np.abs(d + p - f) <= 128).astype(np.float32)
    c["c_l0mask"] = m0.astype(ml_dtypes.bfloat16)
    m1 = np.zeros((4, 128, CCH), np.float32)
    for i in range(4):
        m1[i] = (f >= i * 128 + p).astype(np.float32)
    c["c_l1mask"] = m1.astype(ml_dtypes.bfloat16)
    invf = 10000.0 ** (-np.arange(128, dtype=np.float64) / 128.0)
    fr = np.outer(invf, np.arange(T, dtype=np.float64))
    c["c_cos"] = np.cos(fr).astype(np.float32)
    c["c_sin"] = np.sin(fr).astype(np.float32)
    j = np.arange(CCH, dtype=np.float64)
    gq = np.zeros((H1, 128, CCH), np.float64)
    gk = np.zeros((H1, 128, CCH), np.float64)
    for h in range(H1):
        gq[h] = (GAMMAS[h] ** j)[None, :] / 16.0
        gk[h] = (GAMMAS[h] ** (-j))[None, :]
    c["c_gq"] = gq.astype(np.float32)
    c["c_gk"] = gk.astype(np.float32)
    c["c_ones_row"] = np.ones((1, 128), np.float32)
    c["c_ones_col"] = np.ones((128, 1), np.float32)
    return c


def _host_tt_factors(G1, G2):
    G1 = np.asarray(G1, np.float32)
    G2 = np.asarray(G2, np.float32)
    L = np.zeros((2048, 512), np.float32)
    R = np.zeros((2048, 512), np.float32)
    for cc in range(4):
        for dp in range(32):
            for gp in range(16):
                k = cc * 512 + dp * 16 + gp
                L[k, dp * 16:(dp + 1) * 16] = G2[cc, gp, 0, :]
                R[k, gp::16] = G1[0, :, cc, dp]
    return L, R


def _fm(v):
    v = np.asarray(v, np.float32)
    return np.ascontiguousarray(v.reshape(-1, 128).T)


WSPEC = {
    "l0_wq": (D, D), "l0_wk": (D, D), "l0_wv": (D, D),
    "l1_wq": (D, D), "l1_wk": (D, D),
    "l1_wv": (D, 2 * D), "l1_wg": (D, 2 * D), "l1_wo": (2 * D, D),
    "f_qwt": (D, D), "f_kwt": (D, D), "f_vwt": (D, D), "f_owt": (D, D),
    "wt_r": (512, 512), "wt_i": (512, 512), "wt_ineg": (512, 512),
}
WSPLIT_ORDER = ["l0_wq", "l0_wk", "l0_wv", "l1_wq", "l1_wk", "l1_wv",
                "l1_wg", "l1_wo", "f_qwt", "f_kwt", "f_vwt", "f_owt"]
WSPLIT_GROUP = {"l0_wq": "g0", "l0_wk": "g0", "l0_wv": "g0",
                "l1_wq": "g1", "l1_wk": "g1", "l1_wv": "g1",
                "l1_wg": "g1", "l1_wo": "g1",
                "f_qwt": "g2", "f_kwt": "g2", "f_vwt": "g2", "f_owt": "g2"}


class WStream:

    def __init__(self, bld, nm, want_cols, nslots=2):
        self.bld = bld
        self.nm = nm
        base = getattr(bld, "_ws_ctr", 0)
        bld._ws_ctr = base + nslots
        self.sid = base
        ar = bld.ar
        self.w = want_cols
        self.nslots = nslots
        self.bufs = [ar.bf16(2 * want_cols) for _ in range(nslots)]
        barrier = tuple(getattr(bld, "xs_barrier", ()))
        self.users = [barrier] * nslots
        self.cur = {}

    def fetch(self, slot, kc, c0=0, w=None):
        b = self.bld
        w = self.w if w is None else w
        buf = self.bufs[slot]
        site = b.p.site("wss%d" % (self.sid + slot))
        site.gate = self.users[slot] if self.users[slot] else None
        wait = b.wsp_ready(self.nm)
        src = b.dram["wsp_" + self.nm]
        if w >= 512:
            assert c0 % 512 == 0 and w % 512 == 0
            t2 = site.dma("sync", buf[:, 0:2 * w],
                          src[kc, :, 2 * c0:2 * c0 + 2 * w], wait=wait,
                          first=True)
        else:
            blk, off = c0 // 512, c0 % 512
            assert off + w <= 512
            sr = src[kc, :, blk * 1024:(blk + 1) * 1024].rearrange(
                "p (s c) -> p s c", s=2)[:, :, off:off + w]
            t2 = site.dma("sync",
                          buf[:, 0:2 * w].rearrange("p (s c) -> p s c", s=2),
                          sr, wait=wait, first=True)
        self.cur[slot] = (buf, min(512, w), (t2,))
        return (t2,)

    def use(self, slot, tok):
        self.users[slot] = (tok,)

    def blk(self, slot, a, w):
        buf, bs, _ = self.cur[slot]
        b, off = a // bs, a % bs
        assert off + w <= bs
        c = b * 2 * bs + off
        return buf[:, c:c + w], buf[:, c + bs:c + bs + w]

    def tok(self, slot):
        return self.cur[slot][2]


class WResident:

    def __init__(self, bld, nm, cols, nk, wait=()):
        self.bld = bld
        self.cols = cols
        buf = bld.ar.bf16(nk * 2 * cols)
        site = bld.p.site("wr_" + nm)
        site.gate = tuple(wait)
        for kc in range(nk):
            site.dma("sync", buf[:, kc * 2 * cols:(kc + 1) * 2 * cols],
                     bld.dram["wsp_" + nm][kc, :, :],
                     wait=bld.wsp_ready(nm), first=(kc == 0))
        self.buf = buf
        self.t = (site.tok(),)
        self.cur = {}
        self.nslots = 10 ** 9

    def fetch(self, slot, kc, c0=0, w=None):
        self.cur[slot] = kc
        return self.t

    def use(self, slot, tok):
        pass

    def blk(self, slot, a, w):
        kc = self.cur[slot]
        bs = min(512, self.cols)
        b, off = a // bs, a % bs
        c = kc * 2 * self.cols + b * 2 * bs + off
        return self.buf[:, c:c + w], self.buf[:, c + bs:c + bs + w]



class Builder:
    def __init__(self, nc, passes=3, blocks=("l0a", "l0f", "l1a", "l1f")):
        self.nc = nc
        self.passes = passes
        self.blocks = [b for b in ("l0a", "l0f", "l1a", "l1f") if b in blocks]
        self.p = Prog(nc)
        self.dram = {}
        self.wsp_tok = {}
        self.out_toks = []
        self.bank_guard = [None] * 8
        self._declare_io()
        self._alloc_sbuf()

    def _declare_io(self):
        nc = self.nc
        dr = self.dram

        def din(name, shape, dt=F32):
            dr[name] = nc.dram_tensor(name, shape, dt, kind="ExternalInput").ap()

        dr["xin"] = nc.dram_tensor("xin", [SEQ, NC, 128, T], F32,
                                   kind="ExternalInput").ap()
        dr["out"] = nc.dram_tensor("out", [SEQ, NC, 128, T], F32,
                                   kind="ExternalOutput").ap()
        dr["xs"] = nc.dram_tensor("xs", [SEQ, NC, 128, T], F32).ap()
        din("l0_wq", [D, D]); din("l0_wk", [D, D]); din("l0_wv", [D, D])
        din("l0_bq", [128, NC]); din("l0_bk", [128, NC])
        din("l0_bv_rep", [128, D])
        din("tt_lr", [2048, 512]); din("tt_rr", [2048, 512])
        din("tt_li", [2048, 512]); din("tt_ri", [2048, 512])
        din("l0f_br", [128, 4]); din("l0f_bi", [128, 4])
        din("l1_wq", [D, D]); din("l1_wk", [D, D])
        din("l1_wv", [D, 2 * D]); din("l1_wg", [D, 2 * D])
        din("l1_wo", [2 * D, D])
        din("l1_gnw", [128, 4])
        din("f_qwt", [D, D]); din("f_kwt", [D, D]); din("f_vwt", [D, D])
        din("f_owt", [D, D])
        din("f_nw", [128, NC]); din("f_nb", [128, NC])
        din("f_qb", [128, NC]); din("f_kb", [128, NC])
        din("f_vb_rep", [128, D]); din("f_ob", [128, NC])
        din("c_l0mask", [len(MASK0_DELTAS), 128, 512], BF16)
        din("c_l1mask", [4, 128, CCH], BF16)
        din("c_cos", [128, T]); din("c_sin", [128, T])
        din("c_gq", [H1, 128, CCH]); din("c_gk", [H1, 128, CCH])
        din("c_ones_row", [1, 128]); din("c_ones_col", [128, 1])
        for nm, (rows, cols) in WSPEC.items():
            dr["wsp_" + nm] = nc.dram_tensor(
                "wsp_" + nm, [rows // 128, 128, 2 * cols], BF16).ap()

    def _alloc_sbuf(self):
        nc = self.nc
        used = [0]

        def salloc(name, shape, dtype):
            ap = nc.alloc_sbuf_tensor(name, shape, dtype).ap()
            per = mybir.dt.size(dtype)
            for s in shape[1:]:
                per *= s
            used[0] += per
            return ap

        a = {}
        a["ones_row"] = salloc("ones_row", [1, 128], F32)
        a["ones_col"] = salloc("ones_col", [128, 1], F32)
        a["bias"] = salloc("bias", [128, 80], F32)
        a["eps"] = salloc("eps", [128, 1], F32)
        a["brep"] = salloc("brep", [128, D], F32)
        self.a = a
        avail = 207 * 1024 - used[0]
        self.ar = Arena(nc, (avail // 2) // 64 * 64)
        self.pb = [nc.alloc_psum_tensor(f"pb{i}", [128, 512], F32).ap()
                   for i in range(8)]
        self._post_xc2 = [self.ar.f32(512), self.ar.f32(512)]
        self._post_tmp = [self.ar.f32(512), self.ar.f32(512)]
        self._post_ab = [self.ar.f32(512), self.ar.f32(512)]
        self._post_ab_war = ()
        self._postrows = [self.ar.row(512) for _ in range(5)]
        self._post_prev = ()

    def mm(self, bank, out_ap, lhsT_ap, rhs_ap, start, stop, wait=(),
           inc=False):
        if start and bank is not None and self.bank_guard[bank] is not None:
            wait = tuple(wait) + (self.bank_guard[bank],)
        def b(e, o=out_ap, l=lhsT_ap, r=rhs_ap, st=start, sp=stop):
            return e.matmul(o, l, r, start=st, stop=sp)
        return self.p.comp("tensor", b, wait=wait, inc=inc)

    def guard(self, bank, tok):
        self.bank_guard[bank] = tok

    def v(self, name, *args, wait=(), inc=False, **kw):
        def b(e, n=name, aa=args, k=kw):
            return getattr(e, n)(*aa, **k)
        return self.p.comp("vector", b, wait=wait, inc=inc)

    def s(self, name, *args, wait=(), inc=False, **kw):
        def b(e, n=name, aa=args, k=kw):
            return getattr(e, n)(*aa, **k)
        return self.p.comp("scalar", b, wait=wait, inc=inc)

    def g(self, name, *args, wait=(), inc=False, **kw):
        def b(e, n=name, aa=args, k=kw):
            return getattr(e, n)(*aa, **k)
        return self.p.comp("gpsimd", b, wait=wait, inc=inc)

    def split(self, src_f32, hi_bf, lo_bf, wait=()):
        th = self.s("copy", hi_bf, src_f32, wait=wait, inc=True)
        return self.v("tensor_sub", lo_bf, src_f32, hi_bf, wait=(th,), inc=True)

    def bias_col(self, name, c):
        o = self.boff[name]
        return self.a["bias"][:, o + c:o + c + 1]

    def pairs(self, ah, al, bh, bl):
        if self.passes == 1:
            return [(ah, bh)]
        return [(ah, bh), (ah, bl), (al, bh)]

    def wsp_ready(self, nm):
        t = self.wsp_tok.get(nm)
        return tuple(t) if t else getattr(self, "_wsp_done", ())

    def load_phase_consts(self, spec):
        self._cphase = getattr(self, "_cphase", 0) + 1
        site = self.p.site("cst%d" % self._cphase)
        site.gate = tuple(getattr(self, "xs_barrier", ()))
        first = True
        for key, dn, ncols, dt, ng in spec:
            ap = self.ar.bf16(ncols) if dt == BF16 else self.ar.f32(ncols)
            self.a[key] = ap
            if ng > 1:
                site.dma("sync", ap.rearrange("p (i n) -> p i n", i=ng),
                         self.dram[dn].rearrange("i p n -> p i n"), first=first)
            else:
                site.dma("sync", ap, self.dram[dn], first=first)
            first = False
        self.t_const = (site.tok(),)
        return self.t_const

    def phase0(self):
        p, a, dr = self.p, self.a, self.dram
        c0 = p.site("c0")
        c0.dma("sync", a["ones_row"][:, :], dr["c_ones_row"])
        c0.dma("sync", a["ones_col"][:, :], dr["c_ones_col"])
        self.t_eps = self.g("memset", a["eps"][:, :], EPS, inc=True)
        self.boff = {"l0_bq": 0, "l0_bk": 8, "l0f_br": 16, "l0f_bi": 20,
                     "l1_gnw": 24, "f_nw": 28, "f_nb": 36, "f_qb": 44,
                     "f_kb": 52, "f_ob": 60}
        widths = {"l0_bq": NC, "l0_bk": NC, "l0f_br": 4, "l0f_bi": 4,
                  "l1_gnw": 4, "f_nw": NC, "f_nb": NC, "f_qb": NC,
                  "f_kb": NC, "f_ob": NC}
        for n, o in self.boff.items():
            c0.dma("sync", a["bias"][:, o:o + widths[n]], dr[n])
        self.t_bias = (c0.tok(),)
        self.t_ones = (c0.tok(), c0.tok())
        self.t_const = (c0.tok(),)

        ar = self.ar
        ar.push()
        stL = [ar.f32(512), ar.f32(512)]
        stR = [ar.f32(512), ar.f32(512)]
        wt32 = [ar.f32(2048), ar.f32(2048), ar.f32(2048)]
        last_mm = [None, None]
        ev_all = []
        for idx, (ln, rn) in enumerate([("tt_lr", "tt_rr"), ("tt_li", "tt_ri")]):
            evs = []
            for kc in range(16):
                i = kc % 2
                tts = p.site("tt%d" % i)
                tts.gate = last_mm[i]
                tts.dma("sync", stL[i], dr[ln][kc * 128:(kc + 1) * 128, :],
                        first=True)
                tR = tL = tts.dma("sync", stR[i],
                                  dr[rn][kc * 128:(kc + 1) * 128, :])
                for q in range(4):
                    last_mm[i] = self.mm(
                        q, self.pb[q][:, :], stL[i][:, q * 128:(q + 1) * 128],
                        stR[i], start=(kc == 0), stop=(kc == 15),
                        wait=(tL, tR), inc=True)
            for q in range(4):
                tev = self.v("tensor_copy", wt32[idx][:, q * 512:(q + 1) * 512],
                             self.pb[q][:, :], wait=(last_mm[0], last_mm[1]),
                             inc=True)
                self.guard(q, tev)
                evs.append(tev)
            ev_all.append(tuple(evs))
        tneg = self.s("activation", wt32[2], wt32[1], AF.Identity, scale=-1.0,
                      wait=ev_all[1], inc=True)
        iv = ar.bf16(4096)
        prev = None
        for idx, nm in enumerate(["wt_r", "wt_i", "wt_ineg"]):
            w = ev_all[idx] if idx < 2 else (tneg,)
            wsps = p.site("wspt")
            tprev = prev
            for kc in range(4):
                tl = self.split(wt32[idx][:, kc * 512:(kc + 1) * 512],
                                iv[:, kc * 1024:kc * 1024 + 512],
                                iv[:, kc * 1024 + 512:(kc + 1) * 1024],
                                wait=w + ((tprev,) if tprev else ()))
                tprev = tl
                wsps.dma("gpsimd", dr["wsp_" + nm][kc, :, 0:1024],
                         iv[:, kc * 1024:(kc + 1) * 1024], wait=(tl,),
                         first=(kc == 0))
            wsps.gate = None
            prev = wsps.tok()
            self.wsp_tok[nm] = None
        ar.pop()

        defer = "l0f" in self.blocks
        names0 = [nm for nm in WSPLIT_ORDER
                  if not defer or WSPLIT_GROUP[nm] == "g0"]
        ar.push()
        NPAR = 6
        stg = [ar.f32(1024) for _ in range(NPAR)]
        ivb = [ar.bf16(2048) for _ in range(NPAR)]
        ttgate = (last_mm[0], last_mm[1], p.site("wspt").tok())
        st = self._wsplit_begin(names0, NPAR, stg, ivb, ttgate, "sync")
        self._emit_wsplit(st, 10 ** 9)
        for nm in names0:
            grp = WSPLIT_GROUP[nm]
            self.wsp_tok[nm] = tuple(p.site("wspw_%s_%d" % (grp, j)).tok()
                                     for j in range(NPAR))
        self.wsp_tok["wt_r"] = (p.site("wspt").tok(),)
        self.wsp_tok["wt_i"] = self.wsp_tok["wt_r"]
        self.wsp_tok["wt_ineg"] = self.wsp_tok["wt_r"]
        self._wsp_done = (p.site("wspt").tok(),) + tuple(
            t for t in st["ppt"].values() if t is not None)
        self._p0_ppt = dict(st["ppt"])
        ar.pop()

    def _wsplit_begin(self, names, npar, stg, ivb, gate, loadq):
        for j in range(npar):
            self.p.site("wstg%d" % j).gate = gate
        chunks = []
        for nm in names:
            rows, cols = WSPEC[nm]
            for kc in range(rows // 128):
                for ct in range(max(1, cols // 1024)):
                    chunks.append((nm, kc, ct, min(1024, cols)))
        return {"npar": npar, "stg": stg, "ivb": ivb, "loadq": loadq,
                "chunks": chunks, "i": 0, "ppt": {j: None for j in range(npar)},
                "extra": ()}

    def _emit_wsplit(self, st, n):
        p, dr = self.p, self.dram
        npar = st["npar"]
        while n > 0 and st["chunks"]:
            nm, kc, ct, w = st["chunks"].pop(0)
            grp = WSPLIT_GROUP[nm]
            ii = st["i"] % npar
            st["i"] += 1
            stgs = p.site("wstg%d" % ii)
            wsps = p.site("wspw_%s_%d" % (grp, ii))
            src = dr[nm][kc * 128:(kc + 1) * 128, ct * 1024:ct * 1024 + w]
            tf = stgs.dma(st["loadq"], st["stg"][ii][:, :w], src, first=True)
            pw = (wsps.tok(),) if wsps.count else \
                ((st["ppt"][ii],) if st["ppt"][ii] else st["extra"])
            tl1 = self.split(st["stg"][ii][:, 0:512], st["ivb"][ii][:, 0:512],
                             st["ivb"][ii][:, 512:1024], wait=(tf,) + pw)
            tl2 = self.split(st["stg"][ii][:, 512:1024],
                             st["ivb"][ii][:, 1024:1536],
                             st["ivb"][ii][:, 1536:2048], wait=(tf,) + pw)
            wsps.dma("gpsimd",
                     dr["wsp_" + nm][kc, :, ct * 2048:ct * 2048 + 2 * w],
                     st["ivb"][ii][:, 0:2 * w], wait=(tl1, tl2))
            stgs.gate = (tl1, tl2)
            st["ppt"][ii] = wsps.tok()
            n -= 1

    def ln_stats(self, xap, W, wait, banks=(6, 7, 4, 5), chunks=None,
                 rows=None, evac_wait=(), sq_bufs=None, sq_tok=None,
                 wait_bychunk=None):
        ar, pb = self.ar, self.pb
        chunks = list(range(NC)) if chunks is None else list(chunks)
        pieces = [(0, min(W, 512))] + ([(512, W)] if W > 512 else [])
        if rows is None:
            s1 = ar.row(W)
            s2 = ar.row(W)
        else:
            s1, s2 = rows[0][:, 0:W], rows[1][:, 0:W]
        ar.push()
        sq = sq_bufs if sq_bufs is not None else [ar.f32(512), ar.f32(512)]
        sqt = sq_tok if sq_tok is not None else [None] * len(sq)
        t1 = {}
        t2 = {}
        for ci, c in enumerate(chunks):
            first = ci == 0
            lastc = ci == len(chunks) - 1
            xc = xap[:, c * W:(c + 1) * W]
            i = ci % len(sq)
            wc = wait if wait_bychunk is None \
                else wait + tuple(wait_bychunk.get(c, ()))
            for pi, (p0, p1) in enumerate(pieces):
                bk = banks[2 * pi]
                t1[pi] = self.mm(bk if first else None,
                                 pb[bk][0:1, 0:p1 - p0],
                                 self.a["ones_col"], xc[:, p0:p1],
                                 start=first, stop=lastc,
                                 wait=wc + (self.t_ones[1],), inc=lastc)
                ts = self.s("activation", sq[i][:, 0:p1 - p0], xc[:, p0:p1],
                            AF.Square,
                            wait=wc + ((sqt[i],) if sqt[i] else ()), inc=True)
                bk2 = banks[2 * pi + 1]
                t2[pi] = self.mm(bk2 if first else None,
                                 pb[bk2][0:1, 0:p1 - p0],
                                 self.a["ones_col"], sq[i][:, 0:p1 - p0],
                                 start=first, stop=lastc, wait=(ts,),
                                 inc=True)
                sqt[i] = t2[pi]
        toks = []
        for pi, (p0, p1) in enumerate(pieces):
            e1 = self.s("copy", s1[:, p0:p1], pb[banks[2 * pi]][0:1, 0:p1 - p0],
                        wait=(t1[pi],) + tuple(evac_wait), inc=True)
            self.guard(banks[2 * pi], e1)
            e2 = self.s("copy", s2[:, p0:p1],
                        pb[banks[2 * pi + 1]][0:1, 0:p1 - p0],
                        wait=(t2[pi],), inc=True)
            self.guard(banks[2 * pi + 1], e2)
            toks += [e1, e2]
        ar.pop()
        return s1, s2, tuple(toks)

    def ln_rows(self, s1, s2, W, wait, n=1024, z_row=None, rows=None):
        zr_given = None
        if rows is None:
            mean = self.ar.row(W)
            msq = self.ar.row(W)
            var = self.ar.row(W)
            arow = self.ar.row(W)
            brow = self.ar.row(W)
        elif len(rows) == 6:
            mean, msq, var, arow, brow, zr_given = (r[:, 0:W] for r in rows)
        else:
            mean, msq, var, arow, brow = (r[:, 0:W] for r in rows)
        t0 = self.s("mul", mean, s1, 1.0 / n, wait=wait, inc=True)
        t1 = self.s("mul", msq, s2, 1.0 / n, wait=wait, inc=True)
        if z_row is not None:
            zr = zr_given if zr_given is not None else self.ar.row(W)
            tz = self.v("reciprocal", zr, z_row, wait=wait, inc=True)
            t0 = self.v("tensor_mul", mean, mean, zr, wait=(t0, tz), inc=True)
            tq = self.v("tensor_mul", msq, msq, zr, wait=(t1,), inc=True)
            t1 = self.v("tensor_mul", msq, msq, zr, wait=(tq,), inc=True)
        tm2 = self.v("tensor_mul", var, mean, mean, wait=(t0,), inc=True)
        tv = self.v("tensor_sub", var, msq, var, wait=(t1, tm2), inc=True)
        tsq = self.s("activation", var, var, AF.Sqrt,
                     bias=self.a["eps"][0:1, :], wait=(tv, self.t_eps),
                     inc=True)
        tin = self.v("reciprocal", arow, var, wait=(tsq,), inc=True)
        tb = self.v("tensor_mul", brow, mean, arow, wait=(tin,), inc=True)
        if z_row is not None:
            ta = self.v("tensor_mul", arow, arow, zr, wait=(tin, tb), inc=True)
            return arow, brow, (ta, tb)
        return arow, brow, (tin, tb)

    def replicate(self, row_ap, W, banks, wait):
        out = []
        toks = []
        pieces = [(0, min(W, 512))] + ([(512, W)] if W > 512 else [])
        for (p0, p1), bk in zip(pieces, banks):
            t = self.mm(bk, self.pb[bk][:, 0:p1 - p0], self.a["ones_row"],
                        row_ap[:, p0:p1], start=True, stop=True,
                        wait=tuple(wait) + (self.t_ones[0],), inc=True)
            out.append((self.pb[bk], p0, p1, t))
            toks.append(t)
        return out, tuple(toks)

    def apply_rows(self, dst_hi, dst_lo, xap, W, a_ps, b_ps, wait,
                   affine=None, chunks=None, xn_f32=None, rep_guard=None):
        ar = self.ar
        ar.push()
        tmp = self._post_tmp
        prev = [None, None]
        toks = []
        self._apply_bychunk = {}
        chunks = list(range(NC)) if chunks is None else list(chunks)
        pieces = [(0, min(W, 512))] + ([(512, W)] if W > 512 else [])
        for c in chunks:
            for pi, (p0, p1) in enumerate(pieces):
                pw = p1 - p0
                i = c % 2
                xc = xap[:, c * W + p0: c * W + p1]
                aps = a_ps[pi]
                bps = b_ps[pi]
                t1 = self.v("tensor_tensor", tmp[i][:, 0:pw], xc,
                            aps[0][:, 0:pw], ALU.mult,
                            wait=tuple(wait) + (aps[3], bps[3]) +
                            ((prev[i],) if prev[i] else ()), inc=True)
                if xn_f32 is None:
                    xf = tmp[i][:, 0:pw]
                else:
                    xf = xn_f32[:, c * W + p0: c * W + p1]
                t2 = self.v("tensor_sub", xf, tmp[i][:, 0:pw],
                            bps[0][:, 0:pw], wait=(t1,), inc=True)
                if affine is not None:
                    wcol, bcol = affine
                    t2 = self.v("scalar_tensor_tensor", xf, xf, wcol(c),
                                bcol(c).broadcast_to((128, pw)),
                                ALU.mult, ALU.add, wait=(t2,), inc=True)
                tl = self.split(xf, dst_hi[:, c * W + p0: c * W + p1],
                                dst_lo[:, c * W + p0: c * W + p1], wait=(t2,))
                prev[i] = tl
                toks.append(tl)
                self._apply_bychunk[c] = tl
        ar.pop()
        if rep_guard is not None:
            for pi in range(len(pieces)):
                self.guard(rep_guard[pi][0], toks[-1])
                self.guard(rep_guard[pi][1], toks[-1])
        return tuple(toks)

    def ln_to_splits(self, src, seq, t0, W, xwin, hi, lo, affine=None,
                     row_keep=None, chunks=None):
        tx = self.load_x(src, seq, t0, W, xwin)
        s1, s2, tst = self.ln_stats(xwin, W, tx)
        arow, brow, tr = self.ln_rows(s1, s2, W, tst)
        if row_keep is not None:
            ta = self.s("copy", row_keep[0], arow, wait=tr, inc=True)
            tb = self.s("copy", row_keep[1], brow, wait=tr, inc=True)
            tr = tr + (ta, tb)
        na = 2 if W > 512 else 1
        a_ps, _ = self.replicate(arow, W, (4, 5), tr)
        b_ps, _ = self.replicate(brow, W, (6, 7), tr)
        rg = [(4, 6), (5, 7)][:na]
        txn = self.apply_rows(hi, lo, xwin, W, a_ps, b_ps, tx, affine=affine,
                              rep_guard=rg, chunks=chunks)
        self.p.site("xl" + self.cur_blk).gate = txn[-1]
        self.xn_bychunk = dict(self._apply_bychunk)
        return txn


    class _Front:
        pass

    def front_make(self, blk, maxW, nsq=2):
        ar = self.ar
        fr = Builder._Front()
        fr.blk = blk
        fr.maxW = maxW
        fr.buf = [ar.f32(NC * maxW), ar.f32(NC * maxW)]
        fr.ab_sb = (ar.f32(maxW), ar.f32(maxW))
        fr.ab_war = ()
        fr.sq = [ar.f32(512) for _ in range(nsq)]
        fr.sqt = [None] * nsq
        fr.s1 = ar.row(maxW)
        fr.s2 = ar.row(maxW)
        fr.ab = [(ar.row(maxW), ar.row(maxW)), (ar.row(maxW), ar.row(maxW))]
        fr.state = {}
        fr.load_gate = [None, None]
        fr.site = [self.p.site("fl%d" % i + blk) for i in (0, 1)]
        return fr

    def front_load(self, fr, idx, src, seq, t0, W, gate=None,
                   gate_bychunk=None):
        i = idx % 2
        site = fr.site[i]
        buf = fr.buf[i]
        w2 = tuple(getattr(self, "xs_barrier", ()))
        if gate_bychunk is not None:
            site.gate = None
            for c in range(NC):
                site.dma("gpsimd", buf[:, c * W:(c + 1) * W],
                         src[seq, c, :, t0:t0 + W],
                         wait=w2 + tuple(gate_bychunk.get(c, ())),
                         first=(c == 0))
        else:
            site.gate = gate if gate is not None else fr.load_gate[i]
            for c in range(NC):
                site.dma("gpsimd", buf[:, c * W:(c + 1) * W],
                         src[seq, c, :, t0:t0 + W], wait=w2, first=(c == 0))
        fr.state[idx] = {"seq": seq, "t0": t0, "W": W, "load": (site.tok(),),
                         "buf": buf}

    def front_stats(self, fr, idx, n=1024, extra=()):
        st = fr.state[idx]
        W = st["W"]
        s1, s2, tst = self.ln_stats(st["buf"], W, st["load"] + tuple(extra),
                                    rows=(fr.s1, fr.s2),
                                    evac_wait=getattr(fr, "prev_rowtok", ()),
                                    sq_bufs=fr.sq, sq_tok=fr.sqt)
        arow, brow = fr.ab[idx % 2]
        _, _, tr = self.ln_rows(s1, s2, W, tst, n=n,
                                rows=(s1, s2, brow, arow, brow))
        st["rows"] = (arow, brow)
        st["rowtok"] = tr
        st["stat_evac"] = tst
        fr.prev_rowtok = tr

    def front_prep(self, fr, idx):
        st = fr.state[idx]
        W = st["W"]
        a_ps, _ = self.replicate(st["rows"][0], W, (0, 1), st["rowtok"])
        b_ps, _ = self.replicate(st["rows"][1], W, (2, 3), st["rowtok"])
        sta, stb = [], []
        na = 2 if W > 512 else 1
        for pi in range(na):
            for ps_l, dst, acc, bk in ((a_ps, fr.ab_sb[0], sta, (0, 1)[pi]),
                                       (b_ps, fr.ab_sb[1], stb, (2, 3)[pi])):
                ps, p0, p1, t = ps_l[pi]
                tc = self.s("copy", dst[:, p0:p1], ps[:, 0:p1 - p0],
                            wait=(t,) + tuple(fr.ab_war), inc=True)
                self.guard(bk, tc)
                acc.append((dst[:, p0:p1], p0, p1, tc))
        st["aps"], st["bps"] = sta, stb

    def front_apply(self, fr, idx, hi, lo, affine=None, chunks=None, war=()):
        st = fr.state[idx]
        W = st["W"]
        if "aps" not in st:
            self.front_prep(fr, idx)
        a_ps, b_ps = st["aps"], st["bps"]
        txn = self.apply_rows(hi, lo, st["buf"], W, a_ps, b_ps,
                              st["load"] + tuple(war),
                              affine=affine, chunks=chunks)
        self.xn_bychunk = dict(self._apply_bychunk)
        fr.load_gate[idx % 2] = (txn[-1],) + tuple(st["stat_evac"])
        fr.ab_war = (txn[-1],)
        return txn

    def x_src(self, blk):
        return self.dram["xin"] if self.blocks.index(blk) == 0 else self.dram["xs"]

    def x_dst(self, blk):
        return (self.dram["out"]
                if self.blocks.index(blk) == len(self.blocks) - 1
                else self.dram["xs"])

    def ebarrier(self):
        return ((("e", "tensor"), self.p.ecnt["tensor"]),
                (("e", "vector"), self.p.ecnt["vector"]),
                (("e", "scalar"), self.p.ecnt["scalar"]),
                (("e", "gpsimd"), self.p.ecnt["gpsimd"]))

    def load_x(self, src, seq, t0, W, dst_f32, wait=(), sitename="xl"):
        w2 = tuple(wait) + tuple(getattr(self, "xs_barrier", ()))
        site = self.p.site(sitename + self.cur_blk)
        for c in range(NC):
            site.dma("gpsimd", dst_f32[:, c * W:(c + 1) * W],
                     src[seq, c, :, t0:t0 + W],
                     wait=w2 + (self.ebarrier() if c == 0 else ()),
                     first=(c == 0))
        return (site.tok(),)

    def post_ln_begin(self, blk, seq, tau, o_f32, wait, z_row=None,
                      n_feat=1024, wait_bychunk=None, sq_bufs=None):
        pr = self._postrows
        sw = () if wait_bychunk is not None else tuple(wait)
        s1, s2, tst = self.ln_stats(o_f32, 512, sw,
                                    rows=(pr[0], pr[1]),
                                    evac_wait=self._post_prev,
                                    wait_bychunk=wait_bychunk,
                                    sq_bufs=sq_bufs)
        arow, brow, tr = self.ln_rows(s1, s2, 512,
                                      tst + tuple(self._post_prev),
                                      n=n_feat, z_row=z_row,
                                      rows=(pr[0], pr[1], pr[3], pr[2], pr[3],
                                            pr[4]))
        a_ps, ta_ = self.replicate(arow, 512, (4,), tr)
        b_ps, tb_ = self.replicate(brow, 512, (5,), tr)
        self._post_prev = tuple(ta_) + tuple(tb_)
        ab = self._post_ab
        tca = self.s("copy", ab[0], a_ps[0][0][:, 0:512],
                     wait=ta_ + tuple(self._post_ab_war), inc=True)
        tcb = self.s("copy", ab[1], b_ps[0][0][:, 0:512],
                     wait=tb_ + tuple(self._post_ab_war), inc=True)
        self.guard(4, tca)
        self.guard(5, tcb)
        return dict(blk=blk, seq=seq, tau=tau, o=o_f32, tab=(tca, tcb),
                    wait=tuple(wait), wait_bychunk=wait_bychunk, tst=tst)

    def post_ln_finish(self, ctx):
        blk, seq, tau = ctx["blk"], ctx["seq"], ctx["tau"]
        o_f32 = ctx["o"]
        wbc = ctx["wait_bychunk"]
        ab = self._post_ab
        xc2 = self._post_xc2
        tmp = self._post_tmp
        src = self.x_src(blk)
        dst = self.x_dst(blk)
        stoks = []
        t3s = [None, None]
        t3_by = {}
        t1l = None
        for c in range(NC):
            i = c % 2
            lsite = self.p.site("po%d" % i + blk)
            ssite = self.p.site("ps%d" % i + blk)
            wc = tuple(wbc.get(c, ())) if wbc is not None else ctx["wait"]
            tx = lsite.dma("gpsimd", xc2[i],
                           src[seq, c, :, tau * 512:(tau + 1) * 512],
                           wait=tuple(getattr(self, "xs_barrier", ())) +
                           (ssite.tok(),) + ctx["tab"],
                           first=True)
            lsite.gate = None
            oc = o_f32[:, c * 512:(c + 1) * 512]
            t1 = self.v("tensor_tensor", tmp[i], oc, ab[0],
                        ALU.mult,
                        wait=ctx["tab"] + wc +
                        ((t3s[i],) if t3s[i] else ()), inc=True)
            t1l = t1
            t2 = self.v("tensor_sub", tmp[i], tmp[i], ab[1],
                        wait=(t1,), inc=True)
            t3 = self.v("tensor_add", xc2[i], xc2[i], tmp[i], wait=(t2, tx),
                        inc=True)
            t3s[i] = t3
            t3_by[c] = t3
            ts = ssite.dma("gpsimd",
                           dst[seq, c, :, tau * 512:(tau + 1) * 512],
                           xc2[i], wait=(t3,), first=True)
            ssite.gate = None
            stoks.append(ts)
        self._post_ab_war = (t1l, t3s[0], t3s[1])
        self._last_post_t3 = (t3s[0], t3s[1])
        release = {c: (t3_by[c],) + tuple(ctx["tst"]) for c in range(NC)}
        self.p.site("xl" + blk).gate = (
            t3s[0], t3s[1],
            (("e", "tensor"), self.p.ecnt["tensor"]),
            (("e", "scalar"), self.p.ecnt["scalar"]),
            self.p.site("ps0" + blk).tok(),
            self.p.site("ps1" + blk).tok())
        if dst is self.dram["out"]:
            self.out_toks += stoks
        return release, tuple(stoks)

    def post_ln_add(self, blk, seq, tau, o_f32, wait, z_row=None, n_feat=1024,
                    wait_bychunk=None, sq_bufs=None):
        ctx = self.post_ln_begin(blk, seq, tau, o_f32, wait, z_row=z_row,
                                 n_feat=n_feat, wait_bychunk=wait_bychunk,
                                 sq_bufs=sq_bufs)
        release, stoks = self.post_ln_finish(ctx)
        return stoks

    def proj_fm(self, ws, kcs, xh, xl, xstride, n0, n1, out_chunks, banks,
                xtok, evac, wcol0=0, xtok_bychunk=None, wfetch=None, nsub=1):
        nw = n1 - n0
        out_chunks = list(out_chunks)
        kcs = list(kcs)
        per = (len(out_chunks) + nsub - 1) // nsub
        toks = []
        xtok0 = tuple(xtok)
        for g0 in range(0, len(out_chunks), per):
            gchunks = out_chunks[g0:g0 + per]
            gbanks = banks[g0:g0 + per]
            groups = {}
            for ik, kc in enumerate(kcs):
                slot = ik % ws.nslots
                tf = ws.fetch(slot, kc, c0=wcol0, w=wfetch)
                xtok = (xtok_bychunk[kc],) if xtok_bychunk is not None \
                    else xtok0
                tlast = None
                for j, oc in enumerate(gchunks):
                    wp = ws.blk(slot, oc * 128, 128)
                    xp = (xh[:, kc * xstride + n0: kc * xstride + n1],
                          xl[:, kc * xstride + n0: kc * xstride + n1])
                    prs = self.pairs(wp[0], wp[1], xp[0], xp[1])
                    for ip, (A, Bb) in enumerate(prs):
                        first = ik == 0 and ip == 0
                        last = ik == len(kcs) - 1 and ip == len(prs) - 1
                        need = (j == len(gchunks) - 1 and
                                ip == len(prs) - 1) or last
                        tlast = self.mm(gbanks[j] if first else None,
                                        self.pb[gbanks[j]][:, 0:nw], A, Bb,
                                        start=first, stop=last,
                                        wait=tf + tuple(xtok), inc=need)
                        if last:
                            groups[j] = tlast
                ws.use(slot, tlast)
            for j, oc in enumerate(gchunks):
                te = evac(g0 + j, oc, self.pb[gbanks[j]], (groups[j],))
                self.guard(gbanks[j], te)
                toks.append(te)
        return toks

    def proj_tm(self, ws, kcs, xh, xl, xstride, tchunks, v0, v1, banks, xtok,
                evac, xtok_bychunk=None):
        nw = v1 - v0
        groups = {}
        for ik, kc in enumerate(kcs):
            slot = ik % ws.nslots
            tf = ws.fetch(slot, kc, c0=v0, w=nw)
            whi, wlo = ws.blk(slot, 0, nw)
            if xtok_bychunk is not None:
                xtok = (xtok_bychunk[kc],)
            tlast = None
            for j, tch in enumerate(tchunks):
                xp = (xh[:, kc * xstride + tch * 128: kc * xstride + (tch + 1) * 128],
                      xl[:, kc * xstride + tch * 128: kc * xstride + (tch + 1) * 128])
                prs = self.pairs(xp[0], xp[1], whi, wlo)
                for ip, (A, Bb) in enumerate(prs):
                    first = ik == 0 and ip == 0
                    last = ik == len(kcs) - 1 and ip == len(prs) - 1
                    need = (j == len(tchunks) - 1 and ip == len(prs) - 1) or last
                    tlast = self.mm(banks[j] if first else None,
                                    self.pb[banks[j]][:, 0:nw], A, Bb,
                                    start=first, stop=last,
                                    wait=tf + tuple(xtok), inc=need)
                    if last:
                        groups[j] = tlast
            ws.use(slot, tlast)
        toks = []
        for j, tch in enumerate(tchunks):
            te = evac(j, tch, self.pb[banks[j]], (groups[j],))
            self.guard(banks[j], te)
            toks.append(te)
        return toks

    def evac_split_bias(self, dst_hi, dst_lo, stride, width, bias_name=None,
                        bias_idx=None):
        def evac(j, oc, ps, tok):
            d0 = oc * stride
            if bias_name is not None:
                bc = self.bias_col(bias_name,
                                   oc if bias_idx is None else bias_idx(oc))
                th = self.s("activation", dst_hi[:, d0:d0 + width],
                            ps[:, 0:width], AF.Identity, bias=bc,
                            wait=tok + self.t_bias, inc=True)
                tl = self.v("scalar_tensor_tensor", dst_lo[:, d0:d0 + width],
                            ps[:, 0:width], bc, dst_hi[:, d0:d0 + width],
                            ALU.add, ALU.subtract, wait=(th,) + tok, inc=True)
            else:
                th = self.s("copy", dst_hi[:, d0:d0 + width], ps[:, 0:width],
                            wait=tok, inc=True)
                tl = self.v("tensor_sub", dst_lo[:, d0:d0 + width],
                            ps[:, 0:width], dst_hi[:, d0:d0 + width],
                            wait=(th,), inc=True)
            return tl
        return evac

    def evac_split_brep(self, dst_hi, dst_lo, stride, width, brep_ap, stage2):
        def evac(j, tch, ps, tok):
            st = stage2[j % 2]
            d0 = j * stride
            t0 = self.v("tensor_add", st[:, 0:width], ps[:, 0:width], brep_ap,
                        wait=tok + getattr(self, "_brep_tok", ()), inc=True)
            th = self.s("copy", dst_hi[:, d0:d0 + width], st[:, 0:width],
                        wait=(t0,), inc=True)
            tl = self.v("tensor_sub", dst_lo[:, d0:d0 + width], st[:, 0:width],
                        dst_hi[:, d0:d0 + width], wait=(th,), inc=True)
            return tl
        return evac

    def evac_f32(self, dst, stride, width=None, bias_name=None, bias_idx=None):
        width = stride if width is None else width
        def evac(j, oc, ps, tok):
            d0 = oc * stride
            if bias_name is not None:
                bc = self.bias_col(bias_name,
                                   oc if bias_idx is None else bias_idx(oc))
                return self.s("activation", dst[:, d0:d0 + width],
                              ps[:, 0:width], AF.Identity, bias=bc,
                              wait=tok + self.t_bias, inc=True)
            return self.s("copy", dst[:, d0:d0 + width], ps[:, 0:width],
                          wait=tok, inc=True)
        return evac

    @staticmethod
    def _l0a_bounds(w):
        t0 = max(0, w * 512 - 128)
        return t0, min(T, w * 512 + 640) - t0

    def l0a(self):
        ar = self.ar
        ar.push()
        self._ws_ctr = 0
        self.load_phase_consts([("l0mask", "c_l0mask", 6 * 512, BF16, 6)])
        ws_q = WStream(self, "l0_wq", D)
        ws_k = ws_q
        ws_v = ws_q
        tb = self.p.site("brl0a").dma("sync", self.a["brep"][:, :],
                                      self.dram["l0_bv_rep"])
        self._brep_tok = (tb,)
        src = self.x_src("l0a")
        tiles = [(seq, w) for seq in range(SEQ) for w in range(NTT)]
        fr = self.front_make("l0a", 768)
        for j in range(min(2, len(tiles))):
            t0, W = self._l0a_bounds(tiles[j][1])
            self.front_load(fr, j, src, tiles[j][0], t0, W)
        self.front_stats(fr, 0)
        self.front_prep(fr, 0)
        self._xn_war = ()
        pend = None
        for idx, (seq, w) in enumerate(tiles):
            pend = self._l0a_window(idx, fr, tiles, seq, w, ws_q, ws_k, ws_v,
                                    pend)
        self.post_ln_finish(pend)
        ar.pop()

    def _l0a_window(self, idx, fr, tiles, seq, w, ws_q, ws_k, ws_v, pend):
        p, a, ar, pb = self.p, self.a, self.ar, self.pb
        t0, W = self._l0a_bounds(w)
        qo = w * 512 - t0
        ch0 = t0 // 128
        nch = W // 128
        src = self.x_src("l0a")

        ar.push()
        k_hi = ar.bf16(NC * W)
        k_lo = ar.bf16(NC * W)
        v_hi = ar.bf16(nch * 1024)
        v_lo = ar.bf16(nch * 1024)
        q_hi = ar.bf16(NC * 512)
        q_lo = ar.bf16(NC * 512)
        ar.push()
        xn_hi = ar.bf16(NC * W)
        xn_lo = ar.bf16(NC * W)
        txn = self.front_apply(fr, idx, xn_hi, xn_lo, war=self._xn_war)
        if pend is not None:
            release, _ = self.post_ln_finish(pend)
            if idx + 1 < len(tiles):
                nt0, nW = self._l0a_bounds(tiles[idx + 1][1])
                gb = {}
                for cc in range(NC):
                    t = ()
                    for oc in range((cc * nW) // 512,
                                    min(NC, ((cc + 1) * nW + 511) // 512)):
                        t += release[oc]
                    gb[cc] = t
                self.front_load(fr, idx + 1, src, tiles[idx + 1][0], nt0, nW,
                                gate_bychunk=gb)
        stg2 = self._post_tmp
        ws_k.nm = "l0_wk"
        ktoks = []
        for p0 in range(0, W, 512):
            p1 = min(W, p0 + 512)
            ktoks += self.proj_fm(ws_k, range(NC), xn_hi, xn_lo, W, p0, p1,
                                  range(NC), list(range(8)), txn,
                                  self.evac_split_bias(k_hi[:, p0:],
                                                       k_lo[:, p0:],
                                                       W, p1 - p0, "l0_bk"),
                                  xtok_bychunk=self.xn_bychunk, nsub=1)
        ws_v.nm = "l0_wv"
        vtoks = []
        for v0 in range(0, 1024, 512):
            for j0 in range(0, nch, 6):
                js = list(range(j0, min(nch, j0 + 6)))
                vtoks += self.proj_tm(
                    ws_v, range(NC), xn_hi, xn_lo, W, js, v0, v0 + 512,
                    [i % 8 for i in range(len(js))], txn,
                    self.evac_split_brep(v_hi[:, j0 * 1024 + v0:],
                                         v_lo[:, j0 * 1024 + v0:],
                                         1024, 512,
                                         self.a["brep"][:, v0:v0 + 512], stg2),
                    xtok_bychunk=self.xn_bychunk)
        ws_q.nm = "l0_wq"
        qtoks = self.proj_fm(ws_q, range(NC), xn_hi, xn_lo, W, qo, qo + 512,
                             range(NC), list(range(8)), txn,
                             self.evac_split_bias(q_hi, q_lo, 512, 512,
                                                  "l0_bq"),
                             xtok_bychunk=self.xn_bychunk, nsub=1)
        if idx + 1 < len(tiles):
            self.front_stats(fr, idx + 1)
        ar.pop()

        ar.push()
        zrow = ar.row(512)
        o_f32 = fr.state[idx]["buf"][:, 0:NC * 512]
        ar.push()
        est = ar.f32(512)
        wt_hi = ar.bf16(nch * 512)
        wt_lo = ar.bf16(nch * 512)
        tprev = None
        zt = None
        npass = len(self.pairs(0, 0, 0, 0))
        for j in range(nch):
            bank = j % 2
            gtok = None
            i = 0
            for fc in range(NC):
                for ip, (A, Bb) in enumerate(self.pairs(
                        k_hi[:, fc * W + j * 128: fc * W + (j + 1) * 128],
                        k_lo[:, fc * W + j * 128: fc * W + (j + 1) * 128],
                        q_hi[:, fc * 512:(fc + 1) * 512],
                        q_lo[:, fc * 512:(fc + 1) * 512])):
                    first = i == 0
                    last = i == NC * npass - 1
                    gtok = self.mm(bank if first else None, pb[bank][:, :],
                                   A, Bb, start=first, stop=last,
                                   wait=tuple(ktoks) + tuple(qtoks),
                                   inc=last)
                    i += 1
            te = self.s("activation", est, pb[bank][:, :], AF.Exp,
                        scale=1.0 / 32.0,
                        wait=(gtok,) + ((tprev,) if tprev else ()), inc=True)
            self.guard(bank, te)
            mi = MASK0_DELTAS.index((ch0 + j) * 128 - w * 512)
            tm = self.v("tensor_mul", est, est,
                        a["l0mask"][:, mi * 512:(mi + 1) * 512],
                        wait=(te,) + (self.t_const[0],), inc=True)
            zt = self.mm(2 if j == 0 else None, pb[2][0:1, :],
                         a["ones_col"], est, start=(j == 0),
                         stop=(j == nch - 1), wait=(tm,),
                         inc=(j == nch - 1))
            tprev = self.split(est, wt_hi[:, j * 512:(j + 1) * 512],
                               wt_lo[:, j * 512:(j + 1) * 512], wait=(tm,))
        tz = self.s("copy", zrow, pb[2][0:1, :], wait=(zt, tprev), inc=True)
        self.guard(2, tz)

        otoks = []
        for vb in range(NC):
            bank = vb
            gtok = None
            i = 0
            npass = len(self.pairs(0, 0, 0, 0))
            for j in range(nch):
                for ip, (A, Bb) in enumerate(self.pairs(
                        v_hi[:, j * 1024 + vb * 128: j * 1024 + (vb + 1) * 128],
                        v_lo[:, j * 1024 + vb * 128: j * 1024 + (vb + 1) * 128],
                        wt_hi[:, j * 512:(j + 1) * 512],
                        wt_lo[:, j * 512:(j + 1) * 512])):
                    first = i == 0
                    last = i == nch * npass - 1
                    gtok = self.mm(bank if first else None, pb[bank][:, :],
                                   A, Bb, start=first, stop=last,
                                   wait=(tprev, tz) if first else (), inc=last)
                    i += 1
            te = self.s("copy", o_f32[:, vb * 512:(vb + 1) * 512],
                        pb[bank][:, :], wait=(gtok,), inc=True)
            self.guard(bank, te)
            otoks.append(te)
        if idx + 1 < len(tiles):
            self.front_prep(fr, idx + 1)
        self._xn_war = tuple(ktoks) + tuple(vtoks) + tuple(qtoks)
        pend = self.post_ln_begin(
            "l0a", seq, w, o_f32, tuple(otoks), z_row=zrow,
            wait_bychunk={c: (otoks[c],) for c in range(NC)}, sq_bufs=[est])
        ar.pop()
        ar.pop()
        ar.pop()
        return pend

    def l0f(self):
        ar = self.ar
        ar.push()
        self._ws_ctr = 0
        wrw = tuple(getattr(self, "xs_barrier", ())) + self.ebarrier()
        ws_r = WResident(self, "wt_r", 512, 4, wait=wrw)
        ws_i = WResident(self, "wt_i", 512, 4, wait=wrw)
        ws_n = WResident(self, "wt_ineg", 512, 4, wait=wrw)
        src = self.x_src("l0f")
        tiles = [(seq, tau) for seq in range(SEQ) for tau in range(NTT)]
        fr = self.front_make("l0f", 512)
        self._l0f_psq = [ar.f32(512), ar.f32(512)]
        self._l0f_y = [ar.f32(NC * 512), ar.f32(NC * 512)]
        self._l0f_ywar = {}
        names12 = [nm for nm in WSPLIT_ORDER if WSPLIT_GROUP[nm] != "g0"]
        NP2 = 4
        wst = self._wsplit_begin(
            names12, NP2, [ar.f32(1024) for _ in range(NP2)],
            [ar.bf16(2048) for _ in range(NP2)],
            tuple(getattr(self, "xs_barrier", ())) + self.ebarrier() +
            tuple(self._p0_ppt.get(j) for j in range(6)
                  if self._p0_ppt.get(j) is not None), "sync")
        wst["extra"] = self.ebarrier()
        self._l0f_wst = wst
        for j in range(min(2, len(tiles))):
            self.front_load(fr, j, src, tiles[j][0], tiles[j][1] * 512, 512)
        self.front_stats(fr, 0)
        self.front_prep(fr, 0)
        self._xn_war = ()
        pend = None
        for idx, (seq, tau) in enumerate(tiles):
            pend = self._l0f_tile(idx, fr, tiles, seq, tau, ws_r, ws_i, ws_n,
                                  pend)
        self._emit_wsplit(self._l0f_wst, 10 ** 9)
        p = self.p
        for nm in [n for n in WSPLIT_ORDER if WSPLIT_GROUP[n] != "g0"]:
            grp = WSPLIT_GROUP[nm]
            self.wsp_tok[nm] = tuple(p.site("wspw_%s_%d" % (grp, j)).tok()
                                     for j in range(4))
        self._g12_done = tuple(t for t in self._l0f_wst["ppt"].values()
                               if t is not None)
        self.post_ln_finish(pend)
        ar.pop()

    def _l0f_tile(self, idx, fr, tiles, seq, tau, ws_r, ws_i, ws_n, pend):
        ar = self.ar
        src = self.x_src("l0f")
        ar.push()
        xn_hi = ar.bf16(NC * 512)
        xn_lo = ar.bf16(NC * 512)
        txn = self.front_apply(fr, idx, xn_hi, xn_lo, war=self._xn_war)
        if pend is not None:
            self.post_ln_finish(pend)
            self._l0f_ywar[pend["ypar"]] = (tuple(self._last_post_t3) +
                                            tuple(pend["tst"]))
        if 2 <= idx + 1 < len(tiles):
            self.front_load(fr, idx + 1, src, tiles[idx + 1][0],
                            tiles[idx + 1][1] * 512, 512)
        self._emit_wsplit(self._l0f_wst, 12)
        par = idx % 2
        y = self._l0f_y[par]
        ywar = tuple(self._l0f_ywar.get(par, ()))
        toks = []
        pbk = self.pb
        groups = {}
        npass = len(self.pairs(0, 0, 0, 0))
        for half in range(2):
            for kc in range(4):
                ik = half * 4 + kc
                streams = (ws_r, ws_n) if half == 1 else (ws_r, ws_i)
                sA = ws_r if half == 0 else ws_n
                sB = ws_i if half == 0 else ws_r
                slot = ik % 2
                tfA = sA.fetch(slot, kc)
                tfB = sB.fetch(slot, kc)
                xkc = kc + 4 * half
                tlA = tlB = None
                for j in range(8):
                    isreal = j < 4
                    oc = j if isreal else j - 4
                    whi, wlo = (sA if isreal else sB).blk(slot, oc * 128, 128)
                    prs = self.pairs(whi, wlo,
                                     xn_hi[:, xkc * 512:(xkc + 1) * 512],
                                     xn_lo[:, xkc * 512:(xkc + 1) * 512])
                    for ip, (A, Bb) in enumerate(prs):
                        first = ik == 0 and ip == 0
                        last = ik == 7 and ip == npass - 1
                        tmm = self.mm(j if first else None, pbk[j][:, :],
                                      A, Bb, start=first, stop=last,
                                      wait=tfA + tfB +
                                      (self.xn_bychunk[xkc],), inc=last or
                                      (ip == npass - 1 and j in (3, 7)))
                        if last:
                            groups[j] = tmm
                        if ip == npass - 1:
                            (tlA, tlB) = (tmm, tlB) if isreal else (tlA, tmm)
                sA.use(slot, tlA if tlA is not None else tlB)
                sB.use(slot, tlB if tlB is not None else tlA)
        for j in range(8):
            bias = ("l0f_br", j) if j < 4 else ("l0f_bi", j - 4)
            te = self.s("activation", y[:, j * 512:(j + 1) * 512],
                        pbk[j][:, :], AF.Identity,
                        bias=self.bias_col(bias[0], bias[1]),
                        wait=(groups[j],) + self.t_bias + ywar, inc=True)
            self.guard(j, te)
            toks.append(te)
        if idx + 1 < len(tiles):
            self.front_stats(fr, idx + 1)
            self.front_prep(fr, idx + 1)
        self._xn_war = tuple(toks)
        pend = self.post_ln_begin(
            "l0f", seq, tau, y, tuple(toks),
            wait_bychunk={j: (toks[j],) for j in range(NC)},
            sq_bufs=self._l0f_psq)
        pend["ypar"] = par
        ar.pop()
        return pend

    def l1a(self):
        ar = self.ar
        ar.push()
        self._ws_ctr = 0
        self.load_phase_consts([
            ("l1mask", "c_l1mask", 4 * CCH, BF16, 4),
        ])
        ws_q = WStream(self, "l1_wq", 256, nslots=4)
        ws_k = ws_q
        ws_o = WStream(self, "l1_wo", D)
        ws_v = WStream(self, "l1_wv", 512, nslots=4)
        ws_g = WStream(self, "l1_wg", 512, nslots=4)
        S = ar.f32(H1 * 1024)
        self._S_tok = None
        src = self.x_src("l1a")
        tiles = [(seq, c) for seq in range(SEQ) for c in range(NCH)]
        fr = self.front_make("l1a", 512)
        for j in range(min(2, len(tiles))):
            self.front_load(fr, j, src, tiles[j][0], tiles[j][1] * 512, 512)
        self.front_stats(fr, 0)
        self.front_prep(fr, 0)
        self._xn_war = ()
        pend = None
        for idx, (seq, c) in enumerate(tiles):
            if c == 0:
                tz = self.v("memset", S[:, :], 0.0,
                            wait=((self._S_tok,) if self._S_tok else ()) +
                            tuple(getattr(self, "xs_barrier", ())), inc=True)
                self._S_tok = tz
            pend = self._l1a_chunk(idx, fr, tiles, seq, c, ws_q, ws_k, ws_v,
                                   ws_g, ws_o, S, pend)
        self.post_ln_finish(pend)
        ar.pop()

    def _l1a_chunk(self, idx, fr, tiles, seq, c, ws_q, ws_k, ws_v, ws_g,
                   ws_o, S, pend):
        p, a, ar, pb = self.p, self.a, self.ar, self.pb
        src = self.x_src("l1a")
        ar.push()
        xn_hi = ar.bf16(NC * 512)
        xn_lo = ar.bf16(NC * 512)
        cs = ar.f32(512)
        sn = ar.f32(512)
        css = self.p.site("cs")
        css.gate = getattr(self, "_cs_guard", None)
        css.dma("sync", cs, self.dram["c_cos"][:, c * 512:(c + 1) * 512],
                first=True)
        tsn = css.dma("sync", sn, self.dram["c_sin"][:, c * 512:(c + 1) * 512])
        self._cs = (cs, sn, tsn, tsn)
        txn = self.front_apply(fr, idx, xn_hi, xn_lo, war=self._xn_war)
        if pend is not None:
            release, _ = self.post_ln_finish(pend)
            if idx + 1 < len(tiles):
                self.front_load(fr, idx + 1, src, tiles[idx + 1][0],
                                tiles[idx + 1][1] * 512, 512,
                                gate_bychunk=release)
        xnbc = dict(self.xn_bychunk)
        self._l1a_xnbc = xnbc
        y = fr.state[idx]["buf"][:, 0:NC * 512]
        ytoks = [None] * NC
        for h in range(H1):
            ytoks = self._l1a_head(seq, c, h, xn_hi, xn_lo, txn,
                                   ws_q, ws_k, ws_v, ws_g, ws_o, S, y, ytoks)
            if h == 1 and idx + 1 < len(tiles):
                self.front_stats(fr, idx + 1)
        ytoks = tuple(ytoks)
        if idx + 1 < len(tiles):
            self.front_prep(fr, idx + 1)
        self._xn_war = tuple(self._l1a_lastg)
        pend = self.post_ln_begin(
            "l1a", seq, c, y, tuple(ytoks),
            wait_bychunk={cc: (ytoks[cc],) for cc in range(NC)},
            sq_bufs=[cs, sn])
        self._cs_guard = (self._cs_guard,) + tuple(pend["tst"])
        ar.pop()
        return pend

    def _l1a_head(self, seq, c, h, xn_hi, xn_lo, txn,
                  ws_q, ws_k, ws_v, ws_g, ws_o, S, y, ytoks):
        p, a, ar, pb = self.p, self.a, self.ar, self.pb
        ar.push()
        og_hi = ar.bf16(4 * 512)
        og_lo = ar.bf16(4 * 512)
        qps = {}
        kps = {}
        ws_q.nm = "l1_wq"
        self.proj_fm(ws_q, range(NC), xn_hi, xn_lo, 512, 0, 512,
                     [0, 1], [4, 5], txn, self._evac_keep(qps, [4, 5]),
                     wcol0=2 * h * 128, wfetch=256,
                     xtok_bychunk=self._l1a_xnbc)
        ws_k.nm = "l1_wk"
        self.proj_fm(ws_k, range(NC), xn_hi, xn_lo, 512, 0, 512,
                     [0, 1], [6, 7], txn, self._evac_keep(kps, [6, 7]),
                     wcol0=2 * h * 128, wfetch=256,
                     xtok_bychunk=self._l1a_xnbc)
        qp_hi = ar.bf16(1024)
        qp_lo = ar.bf16(1024)
        kp_hi = ar.bf16(1024)
        kp_lo = ar.bf16(1024)
        stbuf = [(ar.f32(512), ar.f32(512)),
                 (self._post_tmp[0], self._post_tmp[1])]
        cs, sn, tcs, tsn = self._cs
        gq = ar.f32(512)
        gk = ar.f32(512)
        ggs = self.p.site("gg")
        ggs.gate = getattr(self, "_gq_guard", None)
        ggs.dma("sync", gq, self.dram["c_gq"][h], first=True)
        tgk = ggs.dma("sync", gk, self.dram["c_gk"][h])
        tgq = tgk
        rt = {}
        prev_use = getattr(self, "_rope_prev", {})
        for (half, which) in ((0, "q"), (0, "k"), (1, "q"), (1, "k")):
            pstore = qps if which == "q" else kps
            gsc = gq if which == "q" else gk
            dh = qp_hi if which == "q" else kp_hi
            dl = qp_lo if which == "q" else kp_lo
            sA, sB = stbuf[0] if which == "q" else stbuf[1]
            x1 = pstore[0][0][:, 0:512]
            x2 = pstore[1][0][:, 0:512]
            wtok = (pstore[0][1], pstore[1][1], tcs, tsn, tgq, tgk)
            pu = prev_use.get(which)
            if pu is not None:
                wtok = wtok + (pu,)
            ca, cb = (cs, sn) if half == 0 else (cs, sn)
            a_in, b_in = (x1, x2) if half == 0 else (x2, x1)
            t1 = self.v("tensor_mul", sA, a_in, ca, wait=wtok, inc=True)
            t2 = self.v("tensor_mul", sB, b_in, cb, wait=wtok, inc=True)
            op = "tensor_sub" if half == 0 else "tensor_add"
            t3 = self.v(op, sA, sA, sB, wait=(t1, t2), inc=True)
            t4 = self.v("tensor_mul", sA, sA, gsc, wait=(t3,), inc=True)
            t5 = self.split(sA, dh[:, half * 512:(half + 1) * 512],
                            dl[:, half * 512:(half + 1) * 512], wait=(t4,))
            rt[(which, half)] = t5
            prev_use[which] = t5
        self._rope_prev = prev_use
        for pstore in (qps, kps):
            self.guard(pstore[0][2], rt[("q", 1)] if pstore is qps else rt[("k", 1)])
            self.guard(pstore[1][2], rt[("q", 1)] if pstore is qps else rt[("k", 1)])
        rtoks = [rt[("q", 1)], rt[("k", 1)]]
        rope_kdc = {0: (rt[("q", 0)], rt[("k", 0)]),
                    1: (rt[("q", 1)], rt[("k", 1)])}
        self._gq_guard = rtoks[-1]
        self._cs_guard = rtoks[-1]
        ktm_hi = ar.bf16(1024)
        ktm_lo = ar.bf16(1024)
        kts = self.p.site("ktm")
        firstt = True
        for kdc in range(2):
            for tch in range(4):
                for (s_, d_) in ((kp_hi, ktm_hi), (kp_lo, ktm_lo)):
                    kts.dma(
                        "scalar",
                        d_[:, tch * 256 + kdc * 128: tch * 256 + (kdc + 1) * 128],
                        s_[:, kdc * 512 + tch * 128: kdc * 512 + (tch + 1) * 128],
                        wait=(rtoks[1],), first=firstt, transpose=True)
                    firstt = False
        ttoks = [kts.tok()]
        vtm_hi = ar.bf16(4 * 512)
        vtm_lo = ar.bf16(4 * 512)
        stg2 = self._post_tmp
        self._brep_tok = ()
        def evac_v(j, tch, ps, tok):
            th = self.s("copy", vtm_hi[:, j * 512:(j + 1) * 512], ps[:, 0:512],
                        wait=tok, inc=True)
            return self.v("tensor_sub", vtm_lo[:, j * 512:(j + 1) * 512],
                          ps[:, 0:512], vtm_hi[:, j * 512:(j + 1) * 512],
                          wait=(th,), inc=True)
        tv = self.proj_tm(ws_v, range(NC), xn_hi, xn_lo, 512, range(4),
                          h * 512, (h + 1) * 512, [0, 1, 2, 3], txn, evac_v,
                          xtok_bychunk=self._l1a_xnbc)
        wt_hi = ar.bf16(4 * 512)
        wt_lo = ar.bf16(4 * 512)
        est = gq
        stoks = []
        npass = len(self.pairs(0, 0, 0, 0))
        sprev = None
        for sb in range(4):
            bank = 4 + sb
            gtok = None
            i = 0
            for kdc in range(2):
                for ip, (A, Bb) in enumerate(self.pairs(
                        kp_hi[:, kdc * 512 + sb * 128: kdc * 512 + (sb + 1) * 128],
                        kp_lo[:, kdc * 512 + sb * 128: kdc * 512 + (sb + 1) * 128],
                        qp_hi[:, kdc * 512:(kdc + 1) * 512],
                        qp_lo[:, kdc * 512:(kdc + 1) * 512])):
                    first = i == 0
                    last = i == 2 * npass - 1
                    gtok = self.mm(bank if first else None, pb[bank][:, :],
                                   A, Bb, start=first, stop=last,
                                   wait=rope_kdc[kdc], inc=last)
                    i += 1
            tm = self.v("tensor_mul", est, pb[bank][:, :],
                        a["l1mask"][:, sb * 512:(sb + 1) * 512],
                        wait=(gtok,) + self.t_const +
                        ((sprev,) if sprev else ()), inc=True)
            self.guard(bank, tm)
            sprev = self.split(est, wt_hi[:, sb * 512:(sb + 1) * 512],
                               wt_lo[:, sb * 512:(sb + 1) * 512], wait=(tm,))
            stoks.append(sprev)
        self._gq_guard = stoks[-1]
        obanks = [6, 7, 2, 3]
        o32 = ar.f32(4 * 512)
        Sh = S[:, h * 1024:(h + 1) * 1024]
        Ssp_hi = ar.bf16(1024)
        Ssp_lo = ar.bf16(1024)
        tS = None
        if c > 0:
            tS = self.split(Sh, Ssp_hi, Ssp_lo, wait=(self._S_tok,))
        otoks = []
        for vb in range(4):
            bank = obanks[vb]
            gtok = None
            i = 0
            seqs = []
            if c > 0:
                for kdc in range(2):
                    seqs += self.pairs(
                        Ssp_hi[:, kdc * 512 + vb * 128: kdc * 512 + (vb + 1) * 128],
                        Ssp_lo[:, kdc * 512 + vb * 128: kdc * 512 + (vb + 1) * 128],
                        qp_hi[:, kdc * 512:(kdc + 1) * 512],
                        qp_lo[:, kdc * 512:(kdc + 1) * 512])
            for sb in range(4):
                seqs += self.pairs(
                    vtm_hi[:, sb * 512 + vb * 128: sb * 512 + (vb + 1) * 128],
                    vtm_lo[:, sb * 512 + vb * 128: sb * 512 + (vb + 1) * 128],
                    wt_hi[:, sb * 512:(sb + 1) * 512],
                    wt_lo[:, sb * 512:(sb + 1) * 512])
            for i, (A, Bb) in enumerate(seqs):
                first = i == 0
                last = i == len(seqs) - 1
                gtok = self.mm(bank if first else None, pb[bank][:, :], A, Bb,
                               start=first, stop=last,
                               wait=((tS,) if tS else ()) + tuple(stoks) +
                               tuple(tv) + (rtoks[0],), inc=last)
            te = self.s("copy", o32[:, vb * 512:(vb + 1) * 512], pb[bank][:, :],
                        wait=(gtok,), inc=True)
            self.guard(bank, te)
            otoks.append(te)
        gC = GAMMAS[h] ** CCH
        dst_stage = stg2
        sutoks = []
        for kdc in range(2):
            bank = kdc
            gtok = None
            seqs = []
            for sb in range(4):
                seqs += self.pairs(
                    ktm_hi[:, sb * 256 + kdc * 128: sb * 256 + (kdc + 1) * 128],
                    ktm_lo[:, sb * 256 + kdc * 128: sb * 256 + (kdc + 1) * 128],
                    vtm_hi[:, sb * 512:(sb + 1) * 512],
                    vtm_lo[:, sb * 512:(sb + 1) * 512])
            for i, (A, Bb) in enumerate(seqs):
                first = i == 0
                last = i == len(seqs) - 1
                gtok = self.mm(bank if first else None, pb[bank][:, :], A, Bb,
                               start=first, stop=last,
                               wait=tuple(ttoks) + tuple(tv), inc=last)
            tsc = self.s("activation", dst_stage[kdc][:, :], pb[bank][:, :],
                         AF.Identity, scale=gC, wait=(gtok,), inc=True)
            self.guard(bank, tsc)
            tup = self.v("scalar_tensor_tensor",
                         Sh[:, kdc * 512:(kdc + 1) * 512],
                         Sh[:, kdc * 512:(kdc + 1) * 512], float(gC),
                         dst_stage[kdc][:, :], ALU.mult, ALU.add,
                         wait=(tsc,) + (((tS,) if tS else ())), inc=True)
            sutoks.append(tup)
        self._S_tok = sutoks[-1]
        self.p.site("ktm").gate = sutoks[-1]
        s2o = ar.row(512)
        sq = stg2[0]
        t2 = None
        sqprev = None
        for vb in range(4):
            ts = self.s("activation", sq[:, :], o32[:, vb * 512:(vb + 1) * 512],
                        AF.Square, wait=tuple(otoks) + tuple(sutoks) +
                        ((sqprev,) if sqprev else ()), inc=True)
            t2 = self.mm(4 if vb == 0 else None, pb[4][0:1, :],
                         a["ones_col"], sq[:, :], start=(vb == 0),
                         stop=(vb == 3), wait=(ts,), inc=True)
            sqprev = t2
        trow = self.s("activation", s2o, pb[4][0:1, :], AF.Sqrt,
                      scale=1.0 / DV1, bias=self.a["eps"][0:1, :],
                      wait=(t2, self.t_eps), inc=True)
        self.guard(4, trow)
        tinv = self.v("reciprocal", s2o, s2o, wait=(trow,), inc=True)
        rep, trep = self.replicate(s2o, 512, (5,), (tinv,))
        gps = {}
        gtoks = self.proj_fm(ws_g, range(NC), xn_hi, xn_lo, 512, 0, 512,
                             range(4), [0, 1, 6, 7], txn,
                             self._evac_keep(gps, [0, 1, 6, 7]), wcol0=h * 512,
                             xtok_bychunk=self._l1a_xnbc)
        self._l1a_lastg = tuple(gtoks)
        res = []
        for vb in range(4):
            to1 = self.v("scalar_tensor_tensor", sq[:, :],
                         o32[:, vb * 512:(vb + 1) * 512],
                         self.bias_col("l1_gnw", vb), rep[0][0][:, 0:512],
                         ALU.mult, ALU.mult,
                         wait=(trep[0],) + tuple(otoks) + self.t_bias +
                         ((res[-1],) if res else ()), inc=True)
            to2 = self.v("tensor_mul", sq[:, :], sq[:, :],
                         gps[vb][0][:, 0:512], wait=(to1,) + (gps[vb][1],),
                         inc=True)
            tsp = self.split(sq, og_hi[:, vb * 512:(vb + 1) * 512],
                             og_lo[:, vb * 512:(vb + 1) * 512], wait=(to2,))
            self.guard(gps[vb][2], tsp)
            res.append(tsp)
        self.guard(5, res[-1])
        ws_o.nm = "l1_wo"
        npass = len(self.pairs(0, 0, 0, 0))
        newytoks = []
        gt = {}
        for vb in range(4):
            kc = h * 4 + vb
            slot = vb % ws_o.nslots
            tf = ws_o.fetch(slot, kc)
            ulast = None
            for oc in range(NC):
                bank = oc
                whi, wlo = ws_o.blk(slot, oc * 128, 128)
                for ip, (A, Bb) in enumerate(self.pairs(
                        whi, wlo,
                        og_hi[:, vb * 512:(vb + 1) * 512],
                        og_lo[:, vb * 512:(vb + 1) * 512])):
                    gt[bank] = self.mm(bank if vb == 0 and ip == 0 else None,
                                       pb[bank][:, :], A, Bb,
                                       start=(vb == 0 and ip == 0),
                                       stop=(vb == 3 and ip == npass - 1),
                                       wait=tf + (res[vb],),
                                       inc=(ip == npass - 1))
                ulast = gt[bank]
            ws_o.use(slot, ulast)
        for oc in range(NC):
            bank = oc
            if ytoks[oc] is None:
                ty = self.s("copy", y[:, oc * 512:(oc + 1) * 512],
                            pb[bank][:, :], wait=(gt[bank],), inc=True)
            else:
                ty = self.v("tensor_tensor", y[:, oc * 512:(oc + 1) * 512],
                            y[:, oc * 512:(oc + 1) * 512], pb[bank][:, :],
                            ALU.add, wait=(gt[bank], ytoks[oc]), inc=True)
            self.guard(bank, ty)
            newytoks.append(ty)
        ar.pop()
        return newytoks

    def _evac_keep(self, store, banks):
        def evac(j, oc, ps, tok):
            store[j] = (ps, tok[0], banks[j])
            return tok[0]
        return evac

    def l1f(self):
        ar = self.ar
        ar.push()
        self._ws_ctr = 0
        ws_q = WStream(self, "f_qwt", 128, nslots=4)
        ws_k = WStream(self, "f_kwt", 128, nslots=4)
        ws_v = WStream(self, "f_vwt", 128, nslots=4)
        ws_o = WStream(self, "f_owt", D)
        self._l1f_y = [ar.f32(NC * 512), ar.f32(NC * 512)]
        self._l1f_psq = [ar.f32(512), ar.f32(512)]
        self._l1f_y_war = {}
        self._l1f_octr = 0
        tb = self.p.site("brl1f").dma(
            "sync", self.a["brep"][:, :], self.dram["f_vb_rep"],
            wait=tuple(getattr(self, "xs_barrier", ())))
        self._brep_tok = (tb,)
        pend = None
        for seq in range(SEQ):
            pend = self._l1f_seq(seq, ws_q, ws_k, ws_v, ws_o, pend)
        self._l1f_finish(pend)
        ar.pop()

    def _l1f_finish(self, pend):
        release, _ = self.post_ln_finish(pend)
        self._l1f_y_war[pend["ypar"]] = (tuple(self._last_post_t3) +
                                         tuple(pend["tst"]))

    def _l1f_seq(self, seq, ws_q, ws_k, ws_v, ws_o, pend):
        p, a, ar, pb = self.p, self.a, self.ar, self.pb
        src = self.x_src("l1f")
        ar.push()
        off = [0]
        for e in range(NEXP):
            off.append(off[-1] + MCH[e])
        xm_hi = ar.bf16(off[-1] * 512)
        xm_lo = ar.bf16(off[-1] * 512)
        rows = []
        xmtoks = []
        for e in range(NEXP):
            arow_keep = ar.row(512)
            brow_keep = ar.row(512)
            rows.append((arow_keep, brow_keep))
        for e in range(NEXP):
            ar.push()
            xwin = ar.f32(NC * 512)
            txn = self.ln_to_splits(
                src, seq, e * 512, 512, xwin,
                xm_hi[:, off[e] * 512:], xm_lo[:, off[e] * 512:],
                affine=(lambda cc: self.bias_col("f_nw", cc),
                        lambda cc: self.bias_col("f_nb", cc)),
                row_keep=rows[e], chunks=range(MCH[e]))
            xmtoks += list(txn)
            self._l1f_xmbc = getattr(self, "_l1f_xmbc", {})
            self._l1f_xmbc[e] = dict(self.xn_bychunk)
            ar.pop()
        on_hi = ar.bf16(off[-1] * 512)
        on_lo = ar.bf16(off[-1] * 512)
        ontoks = []
        for h in range(H2):
            taus = [tau for tau in range(NEXP) if h < MCH[tau]]
            if not taus:
                continue
            ontoks += self._l1f_head(seq, h, taus, ws_q, ws_k, ws_v,
                                     xm_hi, xm_lo, xmtoks, off, on_hi, on_lo)
        for tau in range(NEXP):
            pend = self._l1f_out(seq, tau, ws_o, on_hi, on_lo, ontoks, off,
                                 rows, pend)
        ar.pop()
        return pend

    def _l1f_head(self, seq, h, taus, ws_q, ws_k, ws_v, xm_hi, xm_lo, xmtoks,
                  off, on_hi, on_lo):
        a, ar, pb = self.a, self.ar, self.pb
        ar.push()
        zrows = {}
        Oacc = {}
        for tau in taus:
            zrows[(h, tau)] = ar.row(512)
            Oacc[(h, tau)] = ar.f32(512)
        qh_hi = ar.bf16(len(taus) * 512)
        qh_lo = ar.bf16(len(taus) * 512)
        kh_hi = ar.bf16(512)
        kh_lo = ar.bf16(512)
        vh_hi = ar.bf16(4 * 128)
        vh_lo = ar.bf16(4 * 128)
        wt_hi = ar.bf16(4 * 512)
        wt_lo = ar.bf16(4 * 512)
        est = ar.f32(512)
        stg2 = [ar.f32(512), ar.f32(128)]
        npass = len(self.pairs(0, 0, 0, 0))
        qtoks = []
        for j, tau in enumerate(taus):
            qh = qh_hi[:, j * 512:]
            qlo = qh_lo[:, j * 512:]
            qtoks += self.proj_fm(
                ws_q, range(MCH[tau]),
                xm_hi[:, off[tau] * 512:], xm_lo[:, off[tau] * 512:],
                512, 0, 512, [0], [0], tuple(xmtoks),
                self.evac_split_bias(qh, qlo, 512, 512, "f_qb",
                                     bias_idx=lambda oc, hh=h: hh),
                wcol0=h * 128, xtok_bychunk=self._l1f_xmbc[tau])
        first_e = True
        ztoks = []
        otoks = []
        for e in range(NEXP):
            ktoks = self.proj_fm(
                ws_k, range(MCH[e]),
                xm_hi[:, off[e] * 512:], xm_lo[:, off[e] * 512:],
                512, 0, 512, [0], [1], tuple(xmtoks),
                self.evac_split_bias(kh_hi, kh_lo, 512, 512, "f_kb",
                                     bias_idx=lambda oc, hh=h: hh),
                wcol0=h * 128, xtok_bychunk=self._l1f_xmbc[e])
            vtoks = self.proj_tm(
                ws_v, range(MCH[e]),
                xm_hi[:, off[e] * 512:], xm_lo[:, off[e] * 512:],
                512, range(4), h * 128, (h + 1) * 128, [2, 3, 6, 7],
                tuple(xmtoks),
                self.evac_split_brep(vh_hi, vh_lo, 128, 128,
                                     a["brep"][:, h * 128:(h + 1) * 128],
                                     stg2),
                xtok_bychunk=self._l1f_xmbc[e])
            for j, tau in enumerate(taus):
                sprev = None
                zt = None
                for sb in range(4):
                    bank = 4 + sb
                    gtok = None
                    for ip, (A, Bb) in enumerate(self.pairs(
                            kh_hi[:, sb * 128:(sb + 1) * 128],
                            kh_lo[:, sb * 128:(sb + 1) * 128],
                            qh_hi[:, j * 512:(j + 1) * 512],
                            qh_lo[:, j * 512:(j + 1) * 512])):
                        first = ip == 0
                        last = ip == npass - 1
                        gtok = self.mm(bank if first else None, pb[bank][:, :],
                                       A, Bb, start=first, stop=last,
                                       wait=tuple(ktoks) + tuple(qtoks),
                                       inc=last)
                    te = self.s("activation", est, pb[bank][:, :], AF.Exp,
                                scale=1.0 / 32.0,
                                wait=(gtok,) + ((sprev,) if sprev else ()) +
                                ((zt,) if zt is not None else ()),
                                inc=True)
                    self.guard(bank, te)
                    zt = self.mm(0 if sb == 0 else None, pb[0][0:1, :],
                                 a["ones_col"], est, start=(sb == 0),
                                 stop=(sb == 3), wait=(te,), inc=True)
                    sprev = self.split(est, wt_hi[:, sb * 512:(sb + 1) * 512],
                                       wt_lo[:, sb * 512:(sb + 1) * 512],
                                       wait=(te,))
                if first_e:
                    tz = self.s("copy", zrows[(h, tau)], pb[0][0:1, :],
                                wait=(zt,), inc=True)
                else:
                    tz = self.v("tensor_tensor", zrows[(h, tau)],
                                zrows[(h, tau)], pb[0][0:1, :], ALU.add,
                                wait=(zt,) + tuple(ztoks[-1:]), inc=True)
                self.guard(0, tz)
                ztoks.append(tz)
                bank = 1
                gtok = None
                seqs = []
                for sb in range(4):
                    seqs += self.pairs(
                        vh_hi[:, sb * 128:(sb + 1) * 128],
                        vh_lo[:, sb * 128:(sb + 1) * 128],
                        wt_hi[:, sb * 512:(sb + 1) * 512],
                        wt_lo[:, sb * 512:(sb + 1) * 512])
                for i, (A, Bb) in enumerate(seqs):
                    gtok = self.mm(bank if i == 0 else None, pb[bank][:, :],
                                   A, Bb, start=(i == 0),
                                   stop=(i == len(seqs) - 1),
                                   wait=(sprev,) + tuple(vtoks),
                                   inc=(i == len(seqs) - 1))
                if first_e:
                    to = self.s("copy", Oacc[(h, tau)], pb[bank][:, :],
                                wait=(gtok,), inc=True)
                else:
                    to = self.v("tensor_tensor", Oacc[(h, tau)],
                                Oacc[(h, tau)], pb[bank][:, :], ALU.add,
                                wait=(gtok,) + tuple(otoks[-1:]), inc=True)
                self.guard(bank, to)
                otoks.append(to)
            first_e = False
        res = []
        for j, tau in enumerate(taus):
            rz = ar.row(512)
            tr = self.v("reciprocal", rz, zrows[(h, tau)],
                        wait=tuple(ztoks), inc=True)
            rep, trep = self.replicate(rz, 512, (0,), (tr,))
            tn = self.v("tensor_mul", est, Oacc[(h, tau)], rep[0][0][:, 0:512],
                        wait=(trep[0],) + tuple(otoks) +
                        ((res[-1],) if res else ()), inc=True)
            self.guard(0, tn)
            tsp = self.split(est, on_hi[:, (off[tau] + h) * 512:
                                        (off[tau] + h + 1) * 512],
                             on_lo[:, (off[tau] + h) * 512:
                                   (off[tau] + h + 1) * 512], wait=(tn,))
            res.append(tsp)
        ar.pop()
        return res

    def _l1f_out(self, seq, tau, ws_o, on_hi, on_lo, ontoks, off, rows, pend):
        a, ar, pb = self.a, self.ar, self.pb
        src = self.x_src("l1f")
        ar.push()
        xt = ar.f32(NC * 512)
        xn4 = ar.f32(NC * 512)
        par = self._l1f_octr % 2
        self._l1f_octr += 1
        y = self._l1f_y[par]
        ywar = tuple(self._l1f_y_war.get(par, ()))
        tx = self.load_x(src, seq, tau * 512, 512, xt)
        arow, brow = rows[tau]
        a_ps, _ = self.replicate(arow, 512, (4,), tuple(ontoks[-1:]))
        b_ps, _ = self.replicate(brow, 512, (5,), tuple(ontoks[-1:]))
        ar.push()
        prev = [None, None]
        xntoks = []
        for c in range(NC):
            i = c % 2
            t1 = self.v("tensor_tensor", xn4[:, c * 512:(c + 1) * 512],
                        xt[:, c * 512:(c + 1) * 512], a_ps[0][0][:, 0:512],
                        ALU.mult, wait=tuple(tx) + (a_ps[0][3], b_ps[0][3]) +
                        ((prev[i],) if prev[i] else ()), inc=True)
            t2 = self.v("tensor_sub", xn4[:, c * 512:(c + 1) * 512],
                        xn4[:, c * 512:(c + 1) * 512], b_ps[0][0][:, 0:512],
                        wait=(t1,), inc=True)
            t3 = self.v("scalar_tensor_tensor", xn4[:, c * 512:(c + 1) * 512],
                        xn4[:, c * 512:(c + 1) * 512],
                        self.bias_col("f_nw", c),
                        self.bias_col("f_nb", c).broadcast_to((128, 512)),
                        ALU.mult, ALU.add, wait=(t2,), inc=True)
            prev[i] = t3
            xntoks.append(t3)
        ar.pop()
        self.guard(4, xntoks[-1])
        self.guard(5, xntoks[-1])
        if pend is not None:
            self._l1f_finish(pend)
        m = MCH[tau]
        npass = len(self.pairs(0, 0, 0, 0))
        gt = {}
        for kc in range(m):
            slot = kc % 2
            tf = ws_o.fetch(slot, kc)
            ulast = None
            for oc in range(m):
                whi, wlo = ws_o.blk(slot, oc * 128, 128)
                for ip, (A, Bb) in enumerate(self.pairs(
                        whi, wlo,
                        on_hi[:, (off[tau] + kc) * 512:(off[tau] + kc + 1) * 512],
                        on_lo[:, (off[tau] + kc) * 512:(off[tau] + kc + 1) * 512])):
                    gt[oc] = self.mm(oc if kc == 0 and ip == 0 else None,
                                     pb[oc][:, :], A, Bb,
                                     start=(kc == 0 and ip == 0),
                                     stop=(kc == m - 1 and ip == npass - 1),
                                     wait=tf + tuple(ontoks),
                                     inc=(ip == npass - 1))
                ulast = gt[oc]
            ws_o.use(slot, ulast)
        ytoks = []
        for oc in range(NC):
            if oc >= m:
                ytoks.append(self.v("tensor_copy",
                                    y[:, oc * 512:(oc + 1) * 512],
                                    xn4[:, oc * 512:(oc + 1) * 512],
                                    wait=(xntoks[oc],) + ywar, inc=True))
                continue
            t1 = self.s("activation", y[:, oc * 512:(oc + 1) * 512],
                        pb[oc][:, :], AF.Identity,
                        bias=self.bias_col("f_ob", oc),
                        wait=(gt[oc],) + self.t_bias + ywar, inc=True)
            self.guard(oc, t1)
            t2 = self.v("tensor_add", y[:, oc * 512:(oc + 1) * 512],
                        y[:, oc * 512:(oc + 1) * 512],
                        xn4[:, oc * 512:(oc + 1) * 512],
                        wait=(t1, xntoks[oc]), inc=True)
            ytoks.append(t2)
        pend = self.post_ln_begin(
            "l1f", seq, tau, y, tuple(ytoks),
            wait_bychunk={oc: (ytoks[oc],) for oc in range(NC)},
            sq_bufs=self._l1f_psq)
        pend["ypar"] = par
        ar.pop()
        return pend

    def finish(self):
        if self.out_toks:
            last = self.out_toks[-1]
            self.p.items["gpsimd"].append((None, self.p._norm("gpsimd", (last,)),
                                           None, False, False))

    def build(self):
        self.cur_blk = "p0"
        self.phase0()
        self.xs_barrier = self._wsp_done
        for blk in self.blocks:
            self.cur_blk = blk
            getattr(self, blk)()
            self.xs_barrier = (self.p.site("ps0" + blk).tok(),
                               self.p.site("ps1" + blk).tok()) + \
                tuple(getattr(self, "_g12_done", ()))
        self.finish()
        self.p.replay()



_CACHE = {}


def _build(passes, blocks):
    key = (passes, tuple(blocks))
    if key not in _CACHE:
        nc = bass.Bass("TRN2", target_bir_lowering=False)
        b = Builder(nc, passes=passes, blocks=blocks)
        b.build()
        _CACHE[key] = (nc, b)
    return _CACHE[key]


def make_inputs(inputs, core):
    c = _host_constants()
    x = np.asarray(inputs["L0_attn_x"])
    xs = x[core * SEQ:(core + 1) * SEQ]
    xin = np.ascontiguousarray(xs.transpose(0, 2, 1)).reshape(SEQ, NC, 128, T)
    L_r, R_r = _host_tt_factors(inputs["L0_ffn_G1_R"], inputs["L0_ffn_G2_R"])
    L_i, R_i = _host_tt_factors(inputs["L0_ffn_G1_I"], inputs["L0_ffn_G2_I"])
    m = {
        "xin": xin,
        "l0_wq": np.asarray(inputs["L0_attn_Wq"]),
        "l0_wk": np.asarray(inputs["L0_attn_Wk"]),
        "l0_wv": np.asarray(inputs["L0_attn_Wv"]),
        "l0_bq": _fm(inputs["L0_attn_bq"]),
        "l0_bk": _fm(inputs["L0_attn_bk"]),
        "l0_bv_rep": np.ascontiguousarray(
            np.broadcast_to(np.asarray(inputs["L0_attn_bv"], np.float32),
                            (128, D))),
        "tt_lr": L_r, "tt_rr": R_r, "tt_li": L_i, "tt_ri": R_i,
        "l0f_br": _fm(inputs["L0_ffn_bias_real"]),
        "l0f_bi": _fm(inputs["L0_ffn_bias_imag"]),
        "l1_wq": np.asarray(inputs["L1_attn_Wq"]),
        "l1_wk": np.asarray(inputs["L1_attn_Wk"]),
        "l1_wv": np.asarray(inputs["L1_attn_Wv"]),
        "l1_wg": np.asarray(inputs["L1_attn_Wg"]),
        "l1_wo": np.asarray(inputs["L1_attn_Wo"]),
        "l1_gnw": _fm(inputs["L1_attn_gn_weight"]),
        "f_qwt": np.ascontiguousarray(np.asarray(inputs["L1_ffn_q_w"]).T),
        "f_kwt": np.ascontiguousarray(np.asarray(inputs["L1_ffn_k_w"]).T),
        "f_vwt": np.ascontiguousarray(np.asarray(inputs["L1_ffn_v_w"]).T),
        "f_owt": np.ascontiguousarray(np.asarray(inputs["L1_ffn_o_w"]).T),
        "f_nw": _fm(inputs["L1_ffn_norm_w"]),
        "f_nb": _fm(inputs["L1_ffn_norm_b"]),
        "f_qb": _fm(inputs["L1_ffn_q_b"]),
        "f_kb": _fm(inputs["L1_ffn_k_b"]),
        "f_vb_rep": np.ascontiguousarray(
            np.broadcast_to(np.asarray(inputs["L1_ffn_v_b"], np.float32),
                            (128, D))),
        "f_ob": _fm(inputs["L1_ffn_o_b"]),
    }
    m.update(c)
    return {k: np.ascontiguousarray(v) for k, v in m.items()}


def kernel(**inputs):
    nc, b = _build(3, ("l0a", "l0f", "l1a", "l1f"))
    in_maps = [make_inputs(inputs, core) for core in range(NCORES)]
    res = run_bass_kernel_spmd(nc, in_maps, core_ids=list(range(NCORES)))
    outs = []
    for core in range(NCORES):
        o = np.asarray(res.results[core]["out"])
        outs.append(o.reshape(SEQ, D, T).transpose(0, 2, 1))
    return np.ascontiguousarray(np.concatenate(outs, axis=0), dtype=np.float32)


# version 21
# speedup vs baseline: 1.0387x; 1.0387x over previous
import math
from contextlib import ExitStack

import numpy as np
import concourse.bass as bass
import concourse.mybir as mybir
from concourse.bass_utils import run_bass_kernel_spmd

F32 = mybir.dt.float32
BF16 = mybir.dt.bfloat16
AF = mybir.ActivationFunctionType
ALU = mybir.AluOpType

NCORES = 8
B, T, D = 16, 2048, 1024
SEQ = B // NCORES
NC = D // 128
NTT = T // 512
EPS = 1e-5

H1 = 4
DK1 = 256
DV1 = 512
GAMMAS = [1.0 - 2.0 ** (-5.0 - h) for h in range(H1)]
CCH = 512
NCH = T // CCH
NEXP = 4
H2 = 8
MS = [D // (2 ** e) for e in range(NEXP)]
MCH = [m // 128 for m in MS]
MASK0_DELTAS = [-128, 0, 128, 256, 384, 512]

ENGS = ("sync", "vector", "scalar", "tensor", "gpsimd")



class Site:

    def __init__(self, prog, name):
        self.prog = prog
        self.key = ("s", name)
        self.count = 0
        self.gate = None

    def dma(self, eng, out_ap, in_ap, wait=(), first=False, transpose=False):
        if first and self.gate:
            g = self.gate
            if (isinstance(g, tuple) and len(g) == 2 and
                    isinstance(g[0], tuple) and isinstance(g[1], int)):
                g = (g,)
            wait = tuple(wait) + tuple(g)
        return self.prog._dma(eng, out_ap, in_ap, self, wait, transpose)

    def tok(self):
        return (self.key, self.count * 16)


class Prog:
    def __init__(self, nc):
        self.nc = nc
        self.items = {e: [] for e in ENGS}
        self.ecnt = {e: 0 for e in ENGS}
        self.waited = {e: {} for e in ENGS}
        self.sites = {}

    def site(self, name):
        if name not in self.sites:
            self.sites[name] = Site(self, name)
        return self.sites[name]

    def comp(self, eng, builder, wait=(), inc=False, drain=True):
        waits = self._norm(eng, wait)
        tok = None
        if inc:
            self.ecnt[eng] += 1
            tok = (("e", eng), self.ecnt[eng])
        self.items[eng].append(
            (builder, waits, tok, False,
             drain and eng in ("vector", "scalar", "gpsimd")))
        return tok

    def _dma(self, eng, out_ap, in_ap, site, wait, transpose):
        waits = self._norm(eng, wait)
        site.count += 1
        tok = (site.key, site.count * 16)
        if transpose:
            def builder(e, o=out_ap, i=in_ap):
                return e.dma_start_transpose(o, i)
        else:
            def builder(e, o=out_ap, i=in_ap):
                return e.dma_start(out=o, in_=i)
        self.items[eng].append((builder, waits, tok, True, False))
        return tok

    def _norm(self, eng, wait):
        out = []
        for t in wait:
            if t is None:
                continue
            key, val = t
            if key[0] == "e" and key[1] == eng:
                continue
            if self.waited[eng].get(key, -1) >= val:
                continue
            self.waited[eng][key] = val
            out.append((key, val))
        return out

    def replay(self):
        nc = self.nc
        es = ExitStack()
        sems = {}
        for e in ENGS:
            sems[("e", e)] = es.enter_context(nc.semaphore(f"se_{e}"))
        for name, s in self.sites.items():
            sems[s.key] = es.enter_context(nc.semaphore(f"ss_{name}"))
        with es:
            with nc.Block() as block:
                def make(name):
                    def run(eng):
                        for builder, waits, tok, is_dma, drain in self.items[name]:
                            for key, val in waits:
                                eng.wait_ge(sems[key], val)
                            if builder is None:
                                continue
                            ins = builder(eng)
                            if tok is not None:
                                ins.then_inc(sems[tok[0]], 16 if is_dma else 1)
                            if drain:
                                eng.drain()
                    return run
                block.sync(make("sync"))
                block.vector(make("vector"))
                block.scalar(make("scalar"))
                block.tensor(make("tensor"))
                block.gpsimd(make("gpsimd"))


class Arena:

    def __init__(self, nc, cols):
        self.base = nc.alloc_sbuf_tensor("arena", [128, cols], BF16).ap()
        self.cols = cols
        self.off = 0
        self.peak = 0
        self.stack = []

    def push(self):
        self.stack.append(self.off)

    def pop(self):
        self.off = self.stack.pop()

    def bf16(self, n):
        o = self._take(n)
        return self.base[:, o:o + n]

    def f32(self, n):
        o = self._take(2 * n, 2)
        return self.base[:, o:o + 2 * n].bitcast(F32)

    def row(self, n):
        o = self._take(2 * n, 2)
        return self.base[0:1, o:o + 2 * n].bitcast(F32)

    def _take(self, n, align=1):
        if self.off % align:
            self.off += align - self.off % align
        o = self.off
        self.off += n
        self.peak = max(self.peak, self.off)
        assert self.off <= self.cols, f"arena overflow {self.off}>{self.cols}"
        return o



def _host_constants():
    import ml_dtypes
    c = {}
    p = np.arange(128)[:, None]
    f = np.arange(512)[None, :]
    m0 = np.zeros((len(MASK0_DELTAS), 128, 512), np.float32)
    for i, d in enumerate(MASK0_DELTAS):
        m0[i] = (np.abs(d + p - f) <= 128).astype(np.float32)
    c["c_l0mask"] = m0.astype(ml_dtypes.bfloat16)
    m1 = np.zeros((4, 128, CCH), np.float32)
    for i in range(4):
        m1[i] = (f >= i * 128 + p).astype(np.float32)
    c["c_l1mask"] = m1.astype(ml_dtypes.bfloat16)
    invf = 10000.0 ** (-np.arange(128, dtype=np.float64) / 128.0)
    fr = np.outer(invf, np.arange(T, dtype=np.float64))
    c["c_cos"] = np.cos(fr).astype(np.float32)
    c["c_sin"] = np.sin(fr).astype(np.float32)
    j = np.arange(CCH, dtype=np.float64)
    gq = np.zeros((H1, 128, CCH), np.float64)
    gk = np.zeros((H1, 128, CCH), np.float64)
    for h in range(H1):
        gq[h] = (GAMMAS[h] ** j)[None, :] / 16.0
        gk[h] = (GAMMAS[h] ** (-j))[None, :]
    c["c_gq"] = gq.astype(np.float32)
    c["c_gk"] = gk.astype(np.float32)
    c["c_ones_row"] = np.ones((1, 128), np.float32)
    c["c_ones_col"] = np.ones((128, 1), np.float32)
    return c


def _host_tt_factors(G1, G2):
    G1 = np.asarray(G1, np.float32)
    G2 = np.asarray(G2, np.float32)
    L = np.zeros((2048, 512), np.float32)
    R = np.zeros((2048, 512), np.float32)
    for cc in range(4):
        for dp in range(32):
            for gp in range(16):
                k = cc * 512 + dp * 16 + gp
                L[k, dp * 16:(dp + 1) * 16] = G2[cc, gp, 0, :]
                R[k, gp::16] = G1[0, :, cc, dp]
    return L, R


def _fm(v):
    v = np.asarray(v, np.float32)
    return np.ascontiguousarray(v.reshape(-1, 128).T)


WSPEC = {
    "l0_wq": (D, D), "l0_wk": (D, D), "l0_wv": (D, D),
    "l1_wq": (D, D), "l1_wk": (D, D),
    "l1_wv": (D, 2 * D), "l1_wg": (D, 2 * D), "l1_wo": (2 * D, D),
    "f_qwt": (D, D), "f_kwt": (D, D), "f_vwt": (D, D), "f_owt": (D, D),
    "wt_r": (512, 512), "wt_i": (512, 512), "wt_ineg": (512, 512),
}
WSPLIT_ORDER = ["l0_wq", "l0_wk", "l0_wv", "l1_wq", "l1_wk", "l1_wv",
                "l1_wg", "l1_wo", "f_qwt", "f_kwt", "f_vwt", "f_owt"]
WSPLIT_GROUP = {"l0_wq": "g0", "l0_wk": "g0", "l0_wv": "g0",
                "l1_wq": "g1", "l1_wk": "g1", "l1_wv": "g1",
                "l1_wg": "g1", "l1_wo": "g1",
                "f_qwt": "g2", "f_kwt": "g2", "f_vwt": "g2", "f_owt": "g2"}


class WStream:

    def __init__(self, bld, nm, want_cols, nslots=2):
        self.bld = bld
        self.nm = nm
        base = getattr(bld, "_ws_ctr", 0)
        bld._ws_ctr = base + nslots
        self.sid = base
        ar = bld.ar
        self.w = want_cols
        self.nslots = nslots
        self.bufs = [ar.bf16(2 * want_cols) for _ in range(nslots)]
        barrier = tuple(getattr(bld, "xs_barrier", ()))
        self.users = [barrier] * nslots
        self.cur = {}

    def fetch(self, slot, kc, c0=0, w=None):
        b = self.bld
        w = self.w if w is None else w
        buf = self.bufs[slot]
        site = b.p.site("wss%d" % (self.sid + slot))
        site.gate = self.users[slot] if self.users[slot] else None
        wait = b.wsp_ready(self.nm)
        src = b.dram["wsp_" + self.nm]
        if w >= 512:
            assert c0 % 512 == 0 and w % 512 == 0
            t2 = site.dma("sync", buf[:, 0:2 * w],
                          src[kc, :, 2 * c0:2 * c0 + 2 * w], wait=wait,
                          first=True)
        else:
            blk, off = c0 // 512, c0 % 512
            assert off + w <= 512
            sr = src[kc, :, blk * 1024:(blk + 1) * 1024].rearrange(
                "p (s c) -> p s c", s=2)[:, :, off:off + w]
            t2 = site.dma("sync",
                          buf[:, 0:2 * w].rearrange("p (s c) -> p s c", s=2),
                          sr, wait=wait, first=True)
        self.cur[slot] = (buf, min(512, w), (t2,))
        return (t2,)

    def use(self, slot, tok):
        self.users[slot] = (tok,)

    def blk(self, slot, a, w):
        buf, bs, _ = self.cur[slot]
        b, off = a // bs, a % bs
        assert off + w <= bs
        c = b * 2 * bs + off
        return buf[:, c:c + w], buf[:, c + bs:c + bs + w]

    def tok(self, slot):
        return self.cur[slot][2]


class WResident:

    def __init__(self, bld, nm, cols, nk, wait=()):
        self.bld = bld
        self.cols = cols
        buf = bld.ar.bf16(nk * 2 * cols)
        site = bld.p.site("wr_" + nm)
        site.gate = tuple(wait)
        for kc in range(nk):
            site.dma("sync", buf[:, kc * 2 * cols:(kc + 1) * 2 * cols],
                     bld.dram["wsp_" + nm][kc, :, :],
                     wait=bld.wsp_ready(nm), first=(kc == 0))
        self.buf = buf
        self.t = (site.tok(),)
        self.cur = {}
        self.nslots = 10 ** 9

    def fetch(self, slot, kc, c0=0, w=None):
        self.cur[slot] = kc
        return self.t

    def use(self, slot, tok):
        pass

    def blk(self, slot, a, w):
        kc = self.cur[slot]
        bs = min(512, self.cols)
        b, off = a // bs, a % bs
        c = kc * 2 * self.cols + b * 2 * bs + off
        return self.buf[:, c:c + w], self.buf[:, c + bs:c + bs + w]



class Builder:
    def __init__(self, nc, passes=3, blocks=("l0a", "l0f", "l1a", "l1f")):
        self.nc = nc
        self.passes = passes
        self.blocks = [b for b in ("l0a", "l0f", "l1a", "l1f") if b in blocks]
        self.p = Prog(nc)
        self.dram = {}
        self.wsp_tok = {}
        self.out_toks = []
        self.bank_guard = [None] * 8
        self._declare_io()
        self._alloc_sbuf()

    def _declare_io(self):
        nc = self.nc
        dr = self.dram

        def din(name, shape, dt=F32):
            dr[name] = nc.dram_tensor(name, shape, dt, kind="ExternalInput").ap()

        dr["xin"] = nc.dram_tensor("xin", [SEQ, NC, 128, T], F32,
                                   kind="ExternalInput").ap()
        dr["out"] = nc.dram_tensor("out", [SEQ, NC, 128, T], F32,
                                   kind="ExternalOutput").ap()
        dr["xs"] = nc.dram_tensor("xs", [SEQ, NC, 128, T], F32).ap()
        din("l0_wq", [D, D]); din("l0_wk", [D, D]); din("l0_wv", [D, D])
        din("l0_bq", [128, NC]); din("l0_bk", [128, NC])
        din("l0_bv_rep", [128, D])
        din("tt_lr", [2048, 512]); din("tt_rr", [2048, 512])
        din("tt_li", [2048, 512]); din("tt_ri", [2048, 512])
        din("l0f_br", [128, 4]); din("l0f_bi", [128, 4])
        din("l1_wq", [D, D]); din("l1_wk", [D, D])
        din("l1_wv", [D, 2 * D]); din("l1_wg", [D, 2 * D])
        din("l1_wo", [2 * D, D])
        din("l1_gnw", [128, 4])
        din("f_qwt", [D, D]); din("f_kwt", [D, D]); din("f_vwt", [D, D])
        din("f_owt", [D, D])
        din("f_nw", [128, NC]); din("f_nb", [128, NC])
        din("f_qb", [128, NC]); din("f_kb", [128, NC])
        din("f_vb_rep", [128, D]); din("f_ob", [128, NC])
        din("c_l0mask", [len(MASK0_DELTAS), 128, 512], BF16)
        din("c_l1mask", [4, 128, CCH], BF16)
        din("c_cos", [128, T]); din("c_sin", [128, T])
        din("c_gq", [H1, 128, CCH]); din("c_gk", [H1, 128, CCH])
        din("c_ones_row", [1, 128]); din("c_ones_col", [128, 1])
        for nm, (rows, cols) in WSPEC.items():
            dr["wsp_" + nm] = nc.dram_tensor(
                "wsp_" + nm, [rows // 128, 128, 2 * cols], BF16).ap()

    def _alloc_sbuf(self):
        nc = self.nc
        used = [0]

        def salloc(name, shape, dtype):
            ap = nc.alloc_sbuf_tensor(name, shape, dtype).ap()
            per = mybir.dt.size(dtype)
            for s in shape[1:]:
                per *= s
            used[0] += per
            return ap

        a = {}
        a["ones_row"] = salloc("ones_row", [1, 128], F32)
        a["ones_col"] = salloc("ones_col", [128, 1], F32)
        a["bias"] = salloc("bias", [128, 80], F32)
        a["eps"] = salloc("eps", [128, 1], F32)
        a["brep"] = salloc("brep", [128, D], F32)
        self.a = a
        avail = 207 * 1024 - used[0]
        self.ar = Arena(nc, (avail // 2) // 64 * 64)
        self.pb = [nc.alloc_psum_tensor(f"pb{i}", [128, 512], F32).ap()
                   for i in range(8)]
        self._post_xc2 = [self.ar.f32(512), self.ar.f32(512)]
        self._post_tmp = [self.ar.f32(512), self.ar.f32(512)]
        self._post_ab = [self.ar.f32(512), self.ar.f32(512)]
        self._post_ab_war = ()
        self._postrows = [self.ar.row(512) for _ in range(5)]
        self._post_prev = ()

    def mm(self, bank, out_ap, lhsT_ap, rhs_ap, start, stop, wait=(),
           inc=False):
        if start and bank is not None and self.bank_guard[bank] is not None:
            wait = tuple(wait) + (self.bank_guard[bank],)
        def b(e, o=out_ap, l=lhsT_ap, r=rhs_ap, st=start, sp=stop):
            return e.matmul(o, l, r, start=st, stop=sp)
        return self.p.comp("tensor", b, wait=wait, inc=inc)

    def guard(self, bank, tok):
        self.bank_guard[bank] = tok

    def v(self, name, *args, wait=(), inc=False, **kw):
        def b(e, n=name, aa=args, k=kw):
            return getattr(e, n)(*aa, **k)
        return self.p.comp("vector", b, wait=wait, inc=inc)

    def s(self, name, *args, wait=(), inc=False, **kw):
        def b(e, n=name, aa=args, k=kw):
            return getattr(e, n)(*aa, **k)
        return self.p.comp("scalar", b, wait=wait, inc=inc)

    def g(self, name, *args, wait=(), inc=False, **kw):
        def b(e, n=name, aa=args, k=kw):
            return getattr(e, n)(*aa, **k)
        return self.p.comp("gpsimd", b, wait=wait, inc=inc)

    def split(self, src_f32, hi_bf, lo_bf, wait=()):
        th = self.s("copy", hi_bf, src_f32, wait=wait, inc=True)
        return self.v("tensor_sub", lo_bf, src_f32, hi_bf, wait=(th,), inc=True)

    def bias_col(self, name, c):
        o = self.boff[name]
        return self.a["bias"][:, o + c:o + c + 1]

    def pairs(self, ah, al, bh, bl):
        if self.passes == 1:
            return [(ah, bh)]
        return [(ah, bh), (ah, bl), (al, bh)]

    def wsp_ready(self, nm):
        t = self.wsp_tok.get(nm)
        return tuple(t) if t else getattr(self, "_wsp_done", ())

    def load_phase_consts(self, spec):
        self._cphase = getattr(self, "_cphase", 0) + 1
        site = self.p.site("cst%d" % self._cphase)
        site.gate = tuple(getattr(self, "xs_barrier", ()))
        first = True
        for key, dn, ncols, dt, ng in spec:
            ap = self.ar.bf16(ncols) if dt == BF16 else self.ar.f32(ncols)
            self.a[key] = ap
            if ng > 1:
                site.dma("sync", ap.rearrange("p (i n) -> p i n", i=ng),
                         self.dram[dn].rearrange("i p n -> p i n"), first=first)
            else:
                site.dma("sync", ap, self.dram[dn], first=first)
            first = False
        self.t_const = (site.tok(),)
        return self.t_const

    def phase0(self):
        p, a, dr = self.p, self.a, self.dram
        c0 = p.site("c0")
        c0.dma("sync", a["ones_row"][:, :], dr["c_ones_row"])
        c0.dma("sync", a["ones_col"][:, :], dr["c_ones_col"])
        self.t_eps = self.g("memset", a["eps"][:, :], EPS, inc=True)
        self.boff = {"l0_bq": 0, "l0_bk": 8, "l0f_br": 16, "l0f_bi": 20,
                     "l1_gnw": 24, "f_nw": 28, "f_nb": 36, "f_qb": 44,
                     "f_kb": 52, "f_ob": 60}
        widths = {"l0_bq": NC, "l0_bk": NC, "l0f_br": 4, "l0f_bi": 4,
                  "l1_gnw": 4, "f_nw": NC, "f_nb": NC, "f_qb": NC,
                  "f_kb": NC, "f_ob": NC}
        for n, o in self.boff.items():
            c0.dma("sync", a["bias"][:, o:o + widths[n]], dr[n])
        self.t_bias = (c0.tok(),)
        self.t_ones = (c0.tok(), c0.tok())
        self.t_const = (c0.tok(),)

        ar = self.ar
        ar.push()
        stL = [ar.f32(512), ar.f32(512)]
        stR = [ar.f32(512), ar.f32(512)]
        wt32 = [ar.f32(2048), ar.f32(2048), ar.f32(2048)]
        last_mm = [None, None]
        ev_all = []
        for idx, (ln, rn) in enumerate([("tt_lr", "tt_rr"), ("tt_li", "tt_ri")]):
            evs = []
            for kc in range(16):
                i = kc % 2
                tts = p.site("tt%d" % i)
                tts.gate = last_mm[i]
                tts.dma("sync", stL[i], dr[ln][kc * 128:(kc + 1) * 128, :],
                        first=True)
                tR = tL = tts.dma("sync", stR[i],
                                  dr[rn][kc * 128:(kc + 1) * 128, :])
                for q in range(4):
                    last_mm[i] = self.mm(
                        q, self.pb[q][:, :], stL[i][:, q * 128:(q + 1) * 128],
                        stR[i], start=(kc == 0), stop=(kc == 15),
                        wait=(tL, tR), inc=True)
            for q in range(4):
                tev = self.v("tensor_copy", wt32[idx][:, q * 512:(q + 1) * 512],
                             self.pb[q][:, :], wait=(last_mm[0], last_mm[1]),
                             inc=True)
                self.guard(q, tev)
                evs.append(tev)
            ev_all.append(tuple(evs))
        tneg = self.s("activation", wt32[2], wt32[1], AF.Identity, scale=-1.0,
                      wait=ev_all[1], inc=True)
        iv = ar.bf16(4096)
        prev = None
        for idx, nm in enumerate(["wt_r", "wt_i", "wt_ineg"]):
            w = ev_all[idx] if idx < 2 else (tneg,)
            wsps = p.site("wspt")
            tprev = prev
            for kc in range(4):
                tl = self.split(wt32[idx][:, kc * 512:(kc + 1) * 512],
                                iv[:, kc * 1024:kc * 1024 + 512],
                                iv[:, kc * 1024 + 512:(kc + 1) * 1024],
                                wait=w + ((tprev,) if tprev else ()))
                tprev = tl
                wsps.dma("gpsimd", dr["wsp_" + nm][kc, :, 0:1024],
                         iv[:, kc * 1024:(kc + 1) * 1024], wait=(tl,),
                         first=(kc == 0))
            wsps.gate = None
            prev = wsps.tok()
            self.wsp_tok[nm] = None
        ar.pop()

        defer = "l0f" in self.blocks
        names0 = [nm for nm in WSPLIT_ORDER
                  if not defer or WSPLIT_GROUP[nm] == "g0"]
        ar.push()
        NPAR = 6
        stg = [ar.f32(1024) for _ in range(NPAR)]
        ivb = [ar.bf16(2048) for _ in range(NPAR)]
        ttgate = (last_mm[0], last_mm[1], p.site("wspt").tok())
        st = self._wsplit_begin(names0, NPAR, stg, ivb, ttgate, "sync")
        self._emit_wsplit(st, 10 ** 9)
        for nm in names0:
            grp = WSPLIT_GROUP[nm]
            self.wsp_tok[nm] = tuple(p.site("wspw_%s_%d" % (grp, j)).tok()
                                     for j in range(NPAR))
        self.wsp_tok["wt_r"] = (p.site("wspt").tok(),)
        self.wsp_tok["wt_i"] = self.wsp_tok["wt_r"]
        self.wsp_tok["wt_ineg"] = self.wsp_tok["wt_r"]
        self._wsp_done = (p.site("wspt").tok(),) + tuple(
            t for t in st["ppt"].values() if t is not None)
        self._p0_ppt = dict(st["ppt"])
        ar.pop()

    def _wsplit_begin(self, names, npar, stg, ivb, gate, loadq):
        for j in range(npar):
            self.p.site("wstg%d" % j).gate = gate
        chunks = []
        for nm in names:
            rows, cols = WSPEC[nm]
            for kc in range(rows // 128):
                for ct in range(max(1, cols // 1024)):
                    chunks.append((nm, kc, ct, min(1024, cols)))
        return {"npar": npar, "stg": stg, "ivb": ivb, "loadq": loadq,
                "chunks": chunks, "i": 0, "ppt": {j: None for j in range(npar)},
                "extra": ()}

    def _emit_wsplit(self, st, n):
        p, dr = self.p, self.dram
        npar = st["npar"]
        while n > 0 and st["chunks"]:
            nm, kc, ct, w = st["chunks"].pop(0)
            grp = WSPLIT_GROUP[nm]
            ii = st["i"] % npar
            st["i"] += 1
            stgs = p.site("wstg%d" % ii)
            wsps = p.site("wspw_%s_%d" % (grp, ii))
            src = dr[nm][kc * 128:(kc + 1) * 128, ct * 1024:ct * 1024 + w]
            tf = stgs.dma(st["loadq"], st["stg"][ii][:, :w], src, first=True)
            pw = (wsps.tok(),) if wsps.count else \
                ((st["ppt"][ii],) if st["ppt"][ii] else st["extra"])
            tl1 = self.split(st["stg"][ii][:, 0:512], st["ivb"][ii][:, 0:512],
                             st["ivb"][ii][:, 512:1024], wait=(tf,) + pw)
            tl2 = self.split(st["stg"][ii][:, 512:1024],
                             st["ivb"][ii][:, 1024:1536],
                             st["ivb"][ii][:, 1536:2048], wait=(tf,) + pw)
            wsps.dma("gpsimd",
                     dr["wsp_" + nm][kc, :, ct * 2048:ct * 2048 + 2 * w],
                     st["ivb"][ii][:, 0:2 * w], wait=(tl1, tl2))
            stgs.gate = (tl1, tl2)
            st["ppt"][ii] = wsps.tok()
            n -= 1

    def ln_stats(self, xap, W, wait, banks=(6, 7, 4, 5), chunks=None,
                 rows=None, evac_wait=(), sq_bufs=None, sq_tok=None,
                 wait_bychunk=None):
        ar, pb = self.ar, self.pb
        chunks = list(range(NC)) if chunks is None else list(chunks)
        pieces = [(0, min(W, 512))] + ([(512, W)] if W > 512 else [])
        if rows is None:
            s1 = ar.row(W)
            s2 = ar.row(W)
        else:
            s1, s2 = rows[0][:, 0:W], rows[1][:, 0:W]
        ar.push()
        sq = sq_bufs if sq_bufs is not None else [ar.f32(512), ar.f32(512)]
        sqt = sq_tok if sq_tok is not None else [None] * len(sq)
        t1 = {}
        t2 = {}
        for ci, c in enumerate(chunks):
            first = ci == 0
            lastc = ci == len(chunks) - 1
            xc = xap[:, c * W:(c + 1) * W]
            i = ci % len(sq)
            wc = wait if wait_bychunk is None \
                else wait + tuple(wait_bychunk.get(c, ()))
            for pi, (p0, p1) in enumerate(pieces):
                bk = banks[2 * pi]
                t1[pi] = self.mm(bk if first else None,
                                 pb[bk][0:1, 0:p1 - p0],
                                 self.a["ones_col"], xc[:, p0:p1],
                                 start=first, stop=lastc,
                                 wait=wc + (self.t_ones[1],), inc=lastc)
                ts = self.s("activation", sq[i][:, 0:p1 - p0], xc[:, p0:p1],
                            AF.Square,
                            wait=wc + ((sqt[i],) if sqt[i] else ()), inc=True)
                bk2 = banks[2 * pi + 1]
                t2[pi] = self.mm(bk2 if first else None,
                                 pb[bk2][0:1, 0:p1 - p0],
                                 self.a["ones_col"], sq[i][:, 0:p1 - p0],
                                 start=first, stop=lastc, wait=(ts,),
                                 inc=True)
                sqt[i] = t2[pi]
        toks = []
        for pi, (p0, p1) in enumerate(pieces):
            e1 = self.s("copy", s1[:, p0:p1], pb[banks[2 * pi]][0:1, 0:p1 - p0],
                        wait=(t1[pi],) + tuple(evac_wait), inc=True)
            self.guard(banks[2 * pi], e1)
            e2 = self.s("copy", s2[:, p0:p1],
                        pb[banks[2 * pi + 1]][0:1, 0:p1 - p0],
                        wait=(t2[pi],), inc=True)
            self.guard(banks[2 * pi + 1], e2)
            toks += [e1, e2]
        ar.pop()
        return s1, s2, tuple(toks)

    def ln_rows(self, s1, s2, W, wait, n=1024, z_row=None, rows=None):
        zr_given = None
        if rows is None:
            mean = self.ar.row(W)
            msq = self.ar.row(W)
            var = self.ar.row(W)
            arow = self.ar.row(W)
            brow = self.ar.row(W)
        elif len(rows) == 6:
            mean, msq, var, arow, brow, zr_given = (r[:, 0:W] for r in rows)
        else:
            mean, msq, var, arow, brow = (r[:, 0:W] for r in rows)
        t0 = self.s("mul", mean, s1, 1.0 / n, wait=wait, inc=True)
        t1 = self.s("mul", msq, s2, 1.0 / n, wait=wait, inc=True)
        if z_row is not None:
            zr = zr_given if zr_given is not None else self.ar.row(W)
            tz = self.v("reciprocal", zr, z_row, wait=wait, inc=True)
            t0 = self.v("tensor_mul", mean, mean, zr, wait=(t0, tz), inc=True)
            tq = self.v("tensor_mul", msq, msq, zr, wait=(t1,), inc=True)
            t1 = self.v("tensor_mul", msq, msq, zr, wait=(tq,), inc=True)
        tm2 = self.v("tensor_mul", var, mean, mean, wait=(t0,), inc=True)
        tv = self.v("tensor_sub", var, msq, var, wait=(t1, tm2), inc=True)
        tsq = self.s("activation", var, var, AF.Sqrt,
                     bias=self.a["eps"][0:1, :], wait=(tv, self.t_eps),
                     inc=True)
        tin = self.v("reciprocal", arow, var, wait=(tsq,), inc=True)
        tb = self.v("tensor_mul", brow, mean, arow, wait=(tin,), inc=True)
        if z_row is not None:
            ta = self.v("tensor_mul", arow, arow, zr, wait=(tin, tb), inc=True)
            return arow, brow, (ta, tb)
        return arow, brow, (tin, tb)

    def replicate(self, row_ap, W, banks, wait):
        out = []
        toks = []
        pieces = [(0, min(W, 512))] + ([(512, W)] if W > 512 else [])
        for (p0, p1), bk in zip(pieces, banks):
            t = self.mm(bk, self.pb[bk][:, 0:p1 - p0], self.a["ones_row"],
                        row_ap[:, p0:p1], start=True, stop=True,
                        wait=tuple(wait) + (self.t_ones[0],), inc=True)
            out.append((self.pb[bk], p0, p1, t))
            toks.append(t)
        return out, tuple(toks)

    def apply_rows(self, dst_hi, dst_lo, xap, W, a_ps, b_ps, wait,
                   affine=None, chunks=None, xn_f32=None, rep_guard=None):
        ar = self.ar
        ar.push()
        tmp = self._post_tmp
        prev = [None, None]
        toks = []
        self._apply_bychunk = {}
        chunks = list(range(NC)) if chunks is None else list(chunks)
        pieces = [(0, min(W, 512))] + ([(512, W)] if W > 512 else [])
        for c in chunks:
            for pi, (p0, p1) in enumerate(pieces):
                pw = p1 - p0
                i = c % 2
                xc = xap[:, c * W + p0: c * W + p1]
                aps = a_ps[pi]
                bps = b_ps[pi]
                t1 = self.v("tensor_tensor", tmp[i][:, 0:pw], xc,
                            aps[0][:, 0:pw], ALU.mult,
                            wait=tuple(wait) + (aps[3], bps[3]) +
                            ((prev[i],) if prev[i] else ()), inc=True)
                if xn_f32 is None:
                    xf = tmp[i][:, 0:pw]
                else:
                    xf = xn_f32[:, c * W + p0: c * W + p1]
                t2 = self.v("tensor_sub", xf, tmp[i][:, 0:pw],
                            bps[0][:, 0:pw], wait=(t1,), inc=True)
                if affine is not None:
                    wcol, bcol = affine
                    t2 = self.v("scalar_tensor_tensor", xf, xf, wcol(c),
                                bcol(c).broadcast_to((128, pw)),
                                ALU.mult, ALU.add, wait=(t2,), inc=True)
                tl = self.split(xf, dst_hi[:, c * W + p0: c * W + p1],
                                dst_lo[:, c * W + p0: c * W + p1], wait=(t2,))
                prev[i] = tl
                toks.append(tl)
                self._apply_bychunk[c] = tl
        ar.pop()
        if rep_guard is not None:
            for pi in range(len(pieces)):
                self.guard(rep_guard[pi][0], toks[-1])
                self.guard(rep_guard[pi][1], toks[-1])
        return tuple(toks)

    def ln_to_splits(self, src, seq, t0, W, xwin, hi, lo, affine=None,
                     row_keep=None, chunks=None):
        tx = self.load_x(src, seq, t0, W, xwin)
        s1, s2, tst = self.ln_stats(xwin, W, tx)
        arow, brow, tr = self.ln_rows(s1, s2, W, tst)
        if row_keep is not None:
            ta = self.s("copy", row_keep[0], arow, wait=tr, inc=True)
            tb = self.s("copy", row_keep[1], brow, wait=tr, inc=True)
            tr = tr + (ta, tb)
        na = 2 if W > 512 else 1
        a_ps, _ = self.replicate(arow, W, (4, 5), tr)
        b_ps, _ = self.replicate(brow, W, (6, 7), tr)
        rg = [(4, 6), (5, 7)][:na]
        txn = self.apply_rows(hi, lo, xwin, W, a_ps, b_ps, tx, affine=affine,
                              rep_guard=rg, chunks=chunks)
        self.p.site("xl" + self.cur_blk).gate = txn[-1]
        self.xn_bychunk = dict(self._apply_bychunk)
        return txn


    class _Front:
        pass

    def front_make(self, blk, maxW, nsq=2):
        ar = self.ar
        fr = Builder._Front()
        fr.blk = blk
        fr.maxW = maxW
        fr.buf = [ar.f32(NC * maxW), ar.f32(NC * maxW)]
        fr.ab_sb = (ar.f32(maxW), ar.f32(maxW))
        fr.ab_war = ()
        fr.sq = [ar.f32(512) for _ in range(nsq)]
        fr.sqt = [None] * nsq
        fr.s1 = ar.row(maxW)
        fr.s2 = ar.row(maxW)
        fr.ab = [(ar.row(maxW), ar.row(maxW)), (ar.row(maxW), ar.row(maxW))]
        fr.state = {}
        fr.load_gate = [None, None]
        fr.site = [self.p.site("fl%d" % i + blk) for i in (0, 1)]
        return fr

    def front_load(self, fr, idx, src, seq, t0, W, gate=None,
                   gate_bychunk=None):
        i = idx % 2
        site = fr.site[i]
        buf = fr.buf[i]
        w2 = tuple(getattr(self, "xs_barrier", ()))
        if gate_bychunk is not None:
            site.gate = None
            for c in range(NC):
                site.dma("gpsimd", buf[:, c * W:(c + 1) * W],
                         src[seq, c, :, t0:t0 + W],
                         wait=w2 + tuple(gate_bychunk.get(c, ())),
                         first=(c == 0))
        else:
            site.gate = gate if gate is not None else fr.load_gate[i]
            for c in range(NC):
                site.dma("gpsimd", buf[:, c * W:(c + 1) * W],
                         src[seq, c, :, t0:t0 + W], wait=w2, first=(c == 0))
        fr.state[idx] = {"seq": seq, "t0": t0, "W": W, "load": (site.tok(),),
                         "buf": buf}

    def front_stats(self, fr, idx, n=1024, extra=()):
        st = fr.state[idx]
        W = st["W"]
        s1, s2, tst = self.ln_stats(st["buf"], W, st["load"] + tuple(extra),
                                    rows=(fr.s1, fr.s2),
                                    evac_wait=getattr(fr, "prev_rowtok", ()),
                                    sq_bufs=fr.sq, sq_tok=fr.sqt)
        arow, brow = fr.ab[idx % 2]
        _, _, tr = self.ln_rows(s1, s2, W, tst, n=n,
                                rows=(s1, s2, brow, arow, brow))
        st["rows"] = (arow, brow)
        st["rowtok"] = tr
        st["stat_evac"] = tst
        fr.prev_rowtok = tr

    def front_prep(self, fr, idx):
        st = fr.state[idx]
        W = st["W"]
        a_ps, _ = self.replicate(st["rows"][0], W, (0, 1), st["rowtok"])
        b_ps, _ = self.replicate(st["rows"][1], W, (2, 3), st["rowtok"])
        sta, stb = [], []
        na = 2 if W > 512 else 1
        for pi in range(na):
            for ps_l, dst, acc, bk in ((a_ps, fr.ab_sb[0], sta, (0, 1)[pi]),
                                       (b_ps, fr.ab_sb[1], stb, (2, 3)[pi])):
                ps, p0, p1, t = ps_l[pi]
                tc = self.s("copy", dst[:, p0:p1], ps[:, 0:p1 - p0],
                            wait=(t,) + tuple(fr.ab_war), inc=True)
                self.guard(bk, tc)
                acc.append((dst[:, p0:p1], p0, p1, tc))
        st["aps"], st["bps"] = sta, stb

    def front_apply(self, fr, idx, hi, lo, affine=None, chunks=None, war=()):
        st = fr.state[idx]
        W = st["W"]
        if "aps" not in st:
            self.front_prep(fr, idx)
        a_ps, b_ps = st["aps"], st["bps"]
        txn = self.apply_rows(hi, lo, st["buf"], W, a_ps, b_ps,
                              st["load"] + tuple(war),
                              affine=affine, chunks=chunks)
        self.xn_bychunk = dict(self._apply_bychunk)
        fr.load_gate[idx % 2] = (txn[-1],) + tuple(st["stat_evac"])
        fr.ab_war = (txn[-1],)
        return txn

    def x_src(self, blk):
        return self.dram["xin"] if self.blocks.index(blk) == 0 else self.dram["xs"]

    def x_dst(self, blk):
        return (self.dram["out"]
                if self.blocks.index(blk) == len(self.blocks) - 1
                else self.dram["xs"])

    def ebarrier(self):
        return ((("e", "tensor"), self.p.ecnt["tensor"]),
                (("e", "vector"), self.p.ecnt["vector"]),
                (("e", "scalar"), self.p.ecnt["scalar"]),
                (("e", "gpsimd"), self.p.ecnt["gpsimd"]))

    def load_x(self, src, seq, t0, W, dst_f32, wait=(), sitename="xl"):
        w2 = tuple(wait) + tuple(getattr(self, "xs_barrier", ()))
        site = self.p.site(sitename + self.cur_blk)
        for c in range(NC):
            site.dma("gpsimd", dst_f32[:, c * W:(c + 1) * W],
                     src[seq, c, :, t0:t0 + W],
                     wait=w2 + (self.ebarrier() if c == 0 else ()),
                     first=(c == 0))
        return (site.tok(),)

    def post_ln_begin(self, blk, seq, tau, o_f32, wait, z_row=None,
                      n_feat=1024, wait_bychunk=None, sq_bufs=None):
        pr = self._postrows
        sw = () if wait_bychunk is not None else tuple(wait)
        s1, s2, tst = self.ln_stats(o_f32, 512, sw,
                                    rows=(pr[0], pr[1]),
                                    evac_wait=self._post_prev,
                                    wait_bychunk=wait_bychunk,
                                    sq_bufs=sq_bufs)
        arow, brow, tr = self.ln_rows(s1, s2, 512,
                                      tst + tuple(self._post_prev),
                                      n=n_feat, z_row=z_row,
                                      rows=(pr[0], pr[1], pr[3], pr[2], pr[3],
                                            pr[4]))
        a_ps, ta_ = self.replicate(arow, 512, (4,), tr)
        b_ps, tb_ = self.replicate(brow, 512, (5,), tr)
        self._post_prev = tuple(ta_) + tuple(tb_)
        ab = self._post_ab
        tca = self.s("copy", ab[0], a_ps[0][0][:, 0:512],
                     wait=ta_ + tuple(self._post_ab_war), inc=True)
        tcb = self.s("copy", ab[1], b_ps[0][0][:, 0:512],
                     wait=tb_ + tuple(self._post_ab_war), inc=True)
        self.guard(4, tca)
        self.guard(5, tcb)
        return dict(blk=blk, seq=seq, tau=tau, o=o_f32, tab=(tca, tcb),
                    wait=tuple(wait), wait_bychunk=wait_bychunk, tst=tst)

    def post_ln_finish(self, ctx):
        blk, seq, tau = ctx["blk"], ctx["seq"], ctx["tau"]
        o_f32 = ctx["o"]
        wbc = ctx["wait_bychunk"]
        ab = self._post_ab
        xc2 = self._post_xc2
        tmp = self._post_tmp
        src = self.x_src(blk)
        dst = self.x_dst(blk)
        stoks = []
        t3s = [None, None]
        t3_by = {}
        t1l = None
        for c in range(NC):
            i = c % 2
            lsite = self.p.site("po%d" % i + blk)
            ssite = self.p.site("ps%d" % i + blk)
            wc = tuple(wbc.get(c, ())) if wbc is not None else ctx["wait"]
            tx = lsite.dma("gpsimd", xc2[i],
                           src[seq, c, :, tau * 512:(tau + 1) * 512],
                           wait=tuple(getattr(self, "xs_barrier", ())) +
                           (ssite.tok(),) + ctx["tab"],
                           first=True)
            lsite.gate = None
            oc = o_f32[:, c * 512:(c + 1) * 512]
            t1 = self.v("tensor_tensor", tmp[i], oc, ab[0],
                        ALU.mult,
                        wait=ctx["tab"] + wc +
                        ((t3s[i],) if t3s[i] else ()), inc=True)
            t1l = t1
            t2 = self.v("tensor_sub", tmp[i], tmp[i], ab[1],
                        wait=(t1,), inc=True)
            t3 = self.v("tensor_add", xc2[i], xc2[i], tmp[i], wait=(t2, tx),
                        inc=True)
            t3s[i] = t3
            t3_by[c] = t3
            ts = ssite.dma("gpsimd",
                           dst[seq, c, :, tau * 512:(tau + 1) * 512],
                           xc2[i], wait=(t3,), first=True)
            ssite.gate = None
            stoks.append(ts)
        self._post_ab_war = (t1l, t3s[0], t3s[1])
        self._last_post_t3 = (t3s[0], t3s[1])
        release = {c: (t3_by[c],) + tuple(ctx["tst"]) for c in range(NC)}
        self.p.site("xl" + blk).gate = (
            t3s[0], t3s[1],
            (("e", "tensor"), self.p.ecnt["tensor"]),
            (("e", "scalar"), self.p.ecnt["scalar"]),
            self.p.site("ps0" + blk).tok(),
            self.p.site("ps1" + blk).tok())
        if dst is self.dram["out"]:
            self.out_toks += stoks
        return release, tuple(stoks)

    def post_ln_add(self, blk, seq, tau, o_f32, wait, z_row=None, n_feat=1024,
                    wait_bychunk=None, sq_bufs=None):
        ctx = self.post_ln_begin(blk, seq, tau, o_f32, wait, z_row=z_row,
                                 n_feat=n_feat, wait_bychunk=wait_bychunk,
                                 sq_bufs=sq_bufs)
        release, stoks = self.post_ln_finish(ctx)
        return stoks

    def proj_fm(self, ws, kcs, xh, xl, xstride, n0, n1, out_chunks, banks,
                xtok, evac, wcol0=0, xtok_bychunk=None, wfetch=None, nsub=1):
        nw = n1 - n0
        out_chunks = list(out_chunks)
        kcs = list(kcs)
        per = (len(out_chunks) + nsub - 1) // nsub
        toks = []
        xtok0 = tuple(xtok)
        for g0 in range(0, len(out_chunks), per):
            gchunks = out_chunks[g0:g0 + per]
            gbanks = banks[g0:g0 + per]
            groups = {}
            for ik, kc in enumerate(kcs):
                slot = ik % ws.nslots
                tf = ws.fetch(slot, kc, c0=wcol0, w=wfetch)
                xtok = (xtok_bychunk[kc],) if xtok_bychunk is not None \
                    else xtok0
                tlast = None
                for j, oc in enumerate(gchunks):
                    wp = ws.blk(slot, oc * 128, 128)
                    xp = (xh[:, kc * xstride + n0: kc * xstride + n1],
                          xl[:, kc * xstride + n0: kc * xstride + n1])
                    prs = self.pairs(wp[0], wp[1], xp[0], xp[1])
                    for ip, (A, Bb) in enumerate(prs):
                        first = ik == 0 and ip == 0
                        last = ik == len(kcs) - 1 and ip == len(prs) - 1
                        need = (j == len(gchunks) - 1 and
                                ip == len(prs) - 1) or last
                        tlast = self.mm(gbanks[j] if first else None,
                                        self.pb[gbanks[j]][:, 0:nw], A, Bb,
                                        start=first, stop=last,
                                        wait=tf + tuple(xtok), inc=need)
                        if last:
                            groups[j] = tlast
                ws.use(slot, tlast)
            for j, oc in enumerate(gchunks):
                te = evac(g0 + j, oc, self.pb[gbanks[j]], (groups[j],))
                self.guard(gbanks[j], te)
                toks.append(te)
        return toks

    def proj_tm(self, ws, kcs, xh, xl, xstride, tchunks, v0, v1, banks, xtok,
                evac, xtok_bychunk=None):
        nw = v1 - v0
        groups = {}
        for ik, kc in enumerate(kcs):
            slot = ik % ws.nslots
            tf = ws.fetch(slot, kc, c0=v0, w=nw)
            whi, wlo = ws.blk(slot, 0, nw)
            if xtok_bychunk is not None:
                xtok = (xtok_bychunk[kc],)
            tlast = None
            for j, tch in enumerate(tchunks):
                xp = (xh[:, kc * xstride + tch * 128: kc * xstride + (tch + 1) * 128],
                      xl[:, kc * xstride + tch * 128: kc * xstride + (tch + 1) * 128])
                prs = self.pairs(xp[0], xp[1], whi, wlo)
                for ip, (A, Bb) in enumerate(prs):
                    first = ik == 0 and ip == 0
                    last = ik == len(kcs) - 1 and ip == len(prs) - 1
                    need = (j == len(tchunks) - 1 and ip == len(prs) - 1) or last
                    tlast = self.mm(banks[j] if first else None,
                                    self.pb[banks[j]][:, 0:nw], A, Bb,
                                    start=first, stop=last,
                                    wait=tf + tuple(xtok), inc=need)
                    if last:
                        groups[j] = tlast
            ws.use(slot, tlast)
        toks = []
        for j, tch in enumerate(tchunks):
            te = evac(j, tch, self.pb[banks[j]], (groups[j],))
            self.guard(banks[j], te)
            toks.append(te)
        return toks

    def evac_split_bias(self, dst_hi, dst_lo, stride, width, bias_name=None,
                        bias_idx=None):
        def evac(j, oc, ps, tok):
            d0 = oc * stride
            if bias_name is not None:
                bc = self.bias_col(bias_name,
                                   oc if bias_idx is None else bias_idx(oc))
                th = self.s("activation", dst_hi[:, d0:d0 + width],
                            ps[:, 0:width], AF.Identity, bias=bc,
                            wait=tok + self.t_bias, inc=True)
                tl = self.v("scalar_tensor_tensor", dst_lo[:, d0:d0 + width],
                            ps[:, 0:width], bc, dst_hi[:, d0:d0 + width],
                            ALU.add, ALU.subtract, wait=(th,) + tok, inc=True)
            else:
                th = self.s("copy", dst_hi[:, d0:d0 + width], ps[:, 0:width],
                            wait=tok, inc=True)
                tl = self.v("tensor_sub", dst_lo[:, d0:d0 + width],
                            ps[:, 0:width], dst_hi[:, d0:d0 + width],
                            wait=(th,), inc=True)
            return tl
        return evac

    def evac_split_brep(self, dst_hi, dst_lo, stride, width, brep_ap, stage2):
        def evac(j, tch, ps, tok):
            st = stage2[j % 2]
            d0 = j * stride
            t0 = self.v("tensor_add", st[:, 0:width], ps[:, 0:width], brep_ap,
                        wait=tok + getattr(self, "_brep_tok", ()), inc=True)
            th = self.s("copy", dst_hi[:, d0:d0 + width], st[:, 0:width],
                        wait=(t0,), inc=True)
            tl = self.v("tensor_sub", dst_lo[:, d0:d0 + width], st[:, 0:width],
                        dst_hi[:, d0:d0 + width], wait=(th,), inc=True)
            return tl
        return evac

    def evac_f32(self, dst, stride, width=None, bias_name=None, bias_idx=None):
        width = stride if width is None else width
        def evac(j, oc, ps, tok):
            d0 = oc * stride
            if bias_name is not None:
                bc = self.bias_col(bias_name,
                                   oc if bias_idx is None else bias_idx(oc))
                return self.s("activation", dst[:, d0:d0 + width],
                              ps[:, 0:width], AF.Identity, bias=bc,
                              wait=tok + self.t_bias, inc=True)
            return self.s("copy", dst[:, d0:d0 + width], ps[:, 0:width],
                          wait=tok, inc=True)
        return evac

    @staticmethod
    def _l0a_bounds(w):
        t0 = max(0, w * 512 - 128)
        return t0, min(T, w * 512 + 640) - t0

    def l0a(self):
        ar = self.ar
        ar.push()
        self._ws_ctr = 0
        self.load_phase_consts([("l0mask", "c_l0mask", 6 * 512, BF16, 6)])
        ws_q = WStream(self, "l0_wq", D)
        ws_k = ws_q
        ws_v = ws_q
        tb = self.p.site("brl0a").dma("sync", self.a["brep"][:, :],
                                      self.dram["l0_bv_rep"])
        self._brep_tok = (tb,)
        src = self.x_src("l0a")
        tiles = [(seq, w) for seq in range(SEQ) for w in range(NTT)]
        fr = self.front_make("l0a", 768)
        for j in range(min(2, len(tiles))):
            t0, W = self._l0a_bounds(tiles[j][1])
            self.front_load(fr, j, src, tiles[j][0], t0, W)
        self.front_stats(fr, 0)
        self.front_prep(fr, 0)
        self._xn_war = ()
        pend = None
        for idx, (seq, w) in enumerate(tiles):
            pend = self._l0a_window(idx, fr, tiles, seq, w, ws_q, ws_k, ws_v,
                                    pend)
        self.post_ln_finish(pend)
        ar.pop()

    def _l0a_window(self, idx, fr, tiles, seq, w, ws_q, ws_k, ws_v, pend):
        p, a, ar, pb = self.p, self.a, self.ar, self.pb
        t0, W = self._l0a_bounds(w)
        qo = w * 512 - t0
        ch0 = t0 // 128
        nch = W // 128
        src = self.x_src("l0a")

        ar.push()
        k_hi = ar.bf16(NC * W)
        k_lo = ar.bf16(NC * W)
        v_hi = ar.bf16(nch * 1024)
        v_lo = ar.bf16(nch * 1024)
        q_hi = ar.bf16(NC * 512)
        q_lo = ar.bf16(NC * 512)
        ar.push()
        xn_hi = ar.bf16(NC * W)
        xn_lo = ar.bf16(NC * W)
        txn = self.front_apply(fr, idx, xn_hi, xn_lo, war=self._xn_war)
        if pend is not None:
            release, _ = self.post_ln_finish(pend)
            if idx + 1 < len(tiles):
                nt0, nW = self._l0a_bounds(tiles[idx + 1][1])
                gb = {}
                for cc in range(NC):
                    t = ()
                    for oc in range((cc * nW) // 512,
                                    min(NC, ((cc + 1) * nW + 511) // 512)):
                        t += release[oc]
                    gb[cc] = t
                self.front_load(fr, idx + 1, src, tiles[idx + 1][0], nt0, nW,
                                gate_bychunk=gb)
        stg2 = self._post_tmp
        ws_k.nm = "l0_wk"
        ktoks = []
        for p0 in range(0, W, 512):
            p1 = min(W, p0 + 512)
            ktoks += self.proj_fm(ws_k, range(NC), xn_hi, xn_lo, W, p0, p1,
                                  range(NC), list(range(8)), txn,
                                  self.evac_split_bias(k_hi[:, p0:],
                                                       k_lo[:, p0:],
                                                       W, p1 - p0, "l0_bk"),
                                  xtok_bychunk=self.xn_bychunk, nsub=1)
        ws_v.nm = "l0_wv"
        vtoks = []
        for v0 in range(0, 1024, 512):
            for j0 in range(0, nch, 6):
                js = list(range(j0, min(nch, j0 + 6)))
                vtoks += self.proj_tm(
                    ws_v, range(NC), xn_hi, xn_lo, W, js, v0, v0 + 512,
                    [i % 8 for i in range(len(js))], txn,
                    self.evac_split_brep(v_hi[:, j0 * 1024 + v0:],
                                         v_lo[:, j0 * 1024 + v0:],
                                         1024, 512,
                                         self.a["brep"][:, v0:v0 + 512], stg2),
                    xtok_bychunk=self.xn_bychunk)
        ws_q.nm = "l0_wq"
        qtoks = self.proj_fm(ws_q, range(NC), xn_hi, xn_lo, W, qo, qo + 512,
                             range(NC), list(range(8)), txn,
                             self.evac_split_bias(q_hi, q_lo, 512, 512,
                                                  "l0_bq"),
                             xtok_bychunk=self.xn_bychunk, nsub=1)
        if idx + 1 < len(tiles):
            self.front_stats(fr, idx + 1)
        ar.pop()

        ar.push()
        zrow = ar.row(512)
        o_f32 = fr.state[idx]["buf"][:, 0:NC * 512]
        ar.push()
        est = ar.f32(512)
        wt_hi = ar.bf16(nch * 512)
        wt_lo = ar.bf16(nch * 512)
        tprev = None
        zt = None
        npass = len(self.pairs(0, 0, 0, 0))
        for j in range(nch):
            bank = j % 2
            gtok = None
            i = 0
            for fc in range(NC):
                for ip, (A, Bb) in enumerate(self.pairs(
                        k_hi[:, fc * W + j * 128: fc * W + (j + 1) * 128],
                        k_lo[:, fc * W + j * 128: fc * W + (j + 1) * 128],
                        q_hi[:, fc * 512:(fc + 1) * 512],
                        q_lo[:, fc * 512:(fc + 1) * 512])):
                    first = i == 0
                    last = i == NC * npass - 1
                    gtok = self.mm(bank if first else None, pb[bank][:, :],
                                   A, Bb, start=first, stop=last,
                                   wait=tuple(ktoks) + tuple(qtoks),
                                   inc=last)
                    i += 1
            te = self.s("activation", est, pb[bank][:, :], AF.Exp,
                        scale=1.0 / 32.0,
                        wait=(gtok,) + ((tprev,) if tprev else ()), inc=True)
            self.guard(bank, te)
            mi = MASK0_DELTAS.index((ch0 + j) * 128 - w * 512)
            tm = self.v("tensor_mul", est, est,
                        a["l0mask"][:, mi * 512:(mi + 1) * 512],
                        wait=(te,) + (self.t_const[0],), inc=True)
            zt = self.mm(2 if j == 0 else None, pb[2][0:1, :],
                         a["ones_col"], est, start=(j == 0),
                         stop=(j == nch - 1), wait=(tm,),
                         inc=(j == nch - 1))
            tprev = self.split(est, wt_hi[:, j * 512:(j + 1) * 512],
                               wt_lo[:, j * 512:(j + 1) * 512], wait=(tm,))
        tz = self.s("copy", zrow, pb[2][0:1, :], wait=(zt, tprev), inc=True)
        self.guard(2, tz)

        otoks = []
        for vb in range(NC):
            bank = vb
            gtok = None
            i = 0
            npass = len(self.pairs(0, 0, 0, 0))
            for j in range(nch):
                for ip, (A, Bb) in enumerate(self.pairs(
                        v_hi[:, j * 1024 + vb * 128: j * 1024 + (vb + 1) * 128],
                        v_lo[:, j * 1024 + vb * 128: j * 1024 + (vb + 1) * 128],
                        wt_hi[:, j * 512:(j + 1) * 512],
                        wt_lo[:, j * 512:(j + 1) * 512])):
                    first = i == 0
                    last = i == nch * npass - 1
                    gtok = self.mm(bank if first else None, pb[bank][:, :],
                                   A, Bb, start=first, stop=last,
                                   wait=(tprev, tz) if first else (), inc=last)
                    i += 1
            te = self.s("copy", o_f32[:, vb * 512:(vb + 1) * 512],
                        pb[bank][:, :], wait=(gtok,), inc=True)
            self.guard(bank, te)
            otoks.append(te)
        if idx + 1 < len(tiles):
            self.front_prep(fr, idx + 1)
        self._xn_war = tuple(ktoks) + tuple(vtoks) + tuple(qtoks)
        pend = self.post_ln_begin(
            "l0a", seq, w, o_f32, tuple(otoks), z_row=zrow,
            wait_bychunk={c: (otoks[c],) for c in range(NC)}, sq_bufs=[est])
        ar.pop()
        ar.pop()
        ar.pop()
        return pend

    def l0f(self):
        ar = self.ar
        ar.push()
        self._ws_ctr = 0
        wrw = tuple(getattr(self, "xs_barrier", ())) + self.ebarrier()
        ws_r = WResident(self, "wt_r", 512, 4, wait=wrw)
        ws_i = WResident(self, "wt_i", 512, 4, wait=wrw)
        ws_n = WResident(self, "wt_ineg", 512, 4, wait=wrw)
        src = self.x_src("l0f")
        tiles = [(seq, tau) for seq in range(SEQ) for tau in range(NTT)]
        fr = self.front_make("l0f", 512)
        self._l0f_psq = [ar.f32(512), ar.f32(512)]
        self._l0f_y = [ar.f32(NC * 512), ar.f32(NC * 512)]
        self._l0f_ywar = {}
        names12 = [nm for nm in WSPLIT_ORDER if WSPLIT_GROUP[nm] != "g0"]
        NP2 = 4
        wst = self._wsplit_begin(
            names12, NP2, [ar.f32(1024) for _ in range(NP2)],
            [ar.bf16(2048) for _ in range(NP2)],
            tuple(getattr(self, "xs_barrier", ())) + self.ebarrier() +
            tuple(self._p0_ppt.get(j) for j in range(6)
                  if self._p0_ppt.get(j) is not None), "sync")
        wst["extra"] = self.ebarrier()
        self._l0f_wst = wst
        for j in range(min(2, len(tiles))):
            self.front_load(fr, j, src, tiles[j][0], tiles[j][1] * 512, 512)
        self.front_stats(fr, 0)
        self.front_prep(fr, 0)
        self._xn_war = ()
        pend = None
        for idx, (seq, tau) in enumerate(tiles):
            pend = self._l0f_tile(idx, fr, tiles, seq, tau, ws_r, ws_i, ws_n,
                                  pend)
        self._emit_wsplit(self._l0f_wst, 10 ** 9)
        p = self.p
        for nm in [n for n in WSPLIT_ORDER if WSPLIT_GROUP[n] != "g0"]:
            grp = WSPLIT_GROUP[nm]
            self.wsp_tok[nm] = tuple(p.site("wspw_%s_%d" % (grp, j)).tok()
                                     for j in range(4))
        self._g12_done = tuple(t for t in self._l0f_wst["ppt"].values()
                               if t is not None)
        self.post_ln_finish(pend)
        ar.pop()

    def _l0f_tile(self, idx, fr, tiles, seq, tau, ws_r, ws_i, ws_n, pend):
        ar = self.ar
        src = self.x_src("l0f")
        ar.push()
        xn_hi = ar.bf16(NC * 512)
        xn_lo = ar.bf16(NC * 512)
        txn = self.front_apply(fr, idx, xn_hi, xn_lo, war=self._xn_war)
        if pend is not None:
            self.post_ln_finish(pend)
            self._l0f_ywar[pend["ypar"]] = (tuple(self._last_post_t3) +
                                            tuple(pend["tst"]))
        if 2 <= idx + 1 < len(tiles):
            self.front_load(fr, idx + 1, src, tiles[idx + 1][0],
                            tiles[idx + 1][1] * 512, 512)
        self._emit_wsplit(self._l0f_wst, 12)
        par = idx % 2
        y = self._l0f_y[par]
        ywar = tuple(self._l0f_ywar.get(par, ()))
        toks = []
        pbk = self.pb
        groups = {}
        npass = len(self.pairs(0, 0, 0, 0))
        for half in range(2):
            for kc in range(4):
                ik = half * 4 + kc
                streams = (ws_r, ws_n) if half == 1 else (ws_r, ws_i)
                sA = ws_r if half == 0 else ws_n
                sB = ws_i if half == 0 else ws_r
                slot = ik % 2
                tfA = sA.fetch(slot, kc)
                tfB = sB.fetch(slot, kc)
                xkc = kc + 4 * half
                tlA = tlB = None
                for j in range(8):
                    isreal = j < 4
                    oc = j if isreal else j - 4
                    whi, wlo = (sA if isreal else sB).blk(slot, oc * 128, 128)
                    prs = self.pairs(whi, wlo,
                                     xn_hi[:, xkc * 512:(xkc + 1) * 512],
                                     xn_lo[:, xkc * 512:(xkc + 1) * 512])
                    for ip, (A, Bb) in enumerate(prs):
                        first = ik == 0 and ip == 0
                        last = ik == 7 and ip == npass - 1
                        tmm = self.mm(j if first else None, pbk[j][:, :],
                                      A, Bb, start=first, stop=last,
                                      wait=tfA + tfB +
                                      (self.xn_bychunk[xkc],), inc=last or
                                      (ip == npass - 1 and j in (3, 7)))
                        if last:
                            groups[j] = tmm
                        if ip == npass - 1:
                            (tlA, tlB) = (tmm, tlB) if isreal else (tlA, tmm)
                sA.use(slot, tlA if tlA is not None else tlB)
                sB.use(slot, tlB if tlB is not None else tlA)
        for j in range(8):
            bias = ("l0f_br", j) if j < 4 else ("l0f_bi", j - 4)
            te = self.s("activation", y[:, j * 512:(j + 1) * 512],
                        pbk[j][:, :], AF.Identity,
                        bias=self.bias_col(bias[0], bias[1]),
                        wait=(groups[j],) + self.t_bias + ywar, inc=True)
            self.guard(j, te)
            toks.append(te)
        if idx + 1 < len(tiles):
            self.front_stats(fr, idx + 1)
            self.front_prep(fr, idx + 1)
        self._xn_war = tuple(toks)
        pend = self.post_ln_begin(
            "l0f", seq, tau, y, tuple(toks),
            wait_bychunk={j: (toks[j],) for j in range(NC)},
            sq_bufs=self._l0f_psq)
        pend["ypar"] = par
        ar.pop()
        return pend

    def l1a(self):
        ar = self.ar
        ar.push()
        self._ws_ctr = 0
        self.load_phase_consts([
            ("l1mask", "c_l1mask", 4 * CCH, BF16, 4),
        ])
        ws_q = WStream(self, "l1_wq", 256, nslots=6)
        ws_k = ws_q
        ws_o = WStream(self, "l1_wo", D)
        ws_v = WStream(self, "l1_wv", 512, nslots=4)
        ws_g = WStream(self, "l1_wg", 512, nslots=4)
        S = ar.f32(H1 * 1024)
        self._S_tok = None
        src = self.x_src("l1a")
        tiles = [(seq, c) for seq in range(SEQ) for c in range(NCH)]
        fr = self.front_make("l1a", 512)
        for j in range(min(2, len(tiles))):
            self.front_load(fr, j, src, tiles[j][0], tiles[j][1] * 512, 512)
        self.front_stats(fr, 0)
        self.front_prep(fr, 0)
        self._xn_war = ()
        pend = None
        for idx, (seq, c) in enumerate(tiles):
            if c == 0:
                tz = self.v("memset", S[:, :], 0.0,
                            wait=((self._S_tok,) if self._S_tok else ()) +
                            tuple(getattr(self, "xs_barrier", ())), inc=True)
                self._S_tok = tz
            pend = self._l1a_chunk(idx, fr, tiles, seq, c, ws_q, ws_k, ws_v,
                                   ws_g, ws_o, S, pend)
        self.post_ln_finish(pend)
        ar.pop()

    def _l1a_chunk(self, idx, fr, tiles, seq, c, ws_q, ws_k, ws_v, ws_g,
                   ws_o, S, pend):
        p, a, ar, pb = self.p, self.a, self.ar, self.pb
        src = self.x_src("l1a")
        ar.push()
        xn_hi = ar.bf16(NC * 512)
        xn_lo = ar.bf16(NC * 512)
        cs = ar.f32(512)
        sn = ar.f32(512)
        css = self.p.site("cs")
        css.gate = getattr(self, "_cs_guard", None)
        css.dma("sync", cs, self.dram["c_cos"][:, c * 512:(c + 1) * 512],
                first=True)
        tsn = css.dma("sync", sn, self.dram["c_sin"][:, c * 512:(c + 1) * 512])
        self._cs = (cs, sn, tsn, tsn)
        txn = self.front_apply(fr, idx, xn_hi, xn_lo, war=self._xn_war)
        if pend is not None:
            release, _ = self.post_ln_finish(pend)
            if idx + 1 < len(tiles):
                self.front_load(fr, idx + 1, src, tiles[idx + 1][0],
                                tiles[idx + 1][1] * 512, 512,
                                gate_bychunk=release)
        xnbc = dict(self.xn_bychunk)
        self._l1a_xnbc = xnbc
        y = fr.state[idx]["buf"][:, 0:NC * 512]
        ytoks = [None] * NC
        for h in range(H1):
            ytoks = self._l1a_head(seq, c, h, xn_hi, xn_lo, txn,
                                   ws_q, ws_k, ws_v, ws_g, ws_o, S, y, ytoks)
            if h == 1 and idx + 1 < len(tiles):
                self.front_stats(fr, idx + 1)
        ytoks = tuple(ytoks)
        if idx + 1 < len(tiles):
            self.front_prep(fr, idx + 1)
        self._xn_war = tuple(self._l1a_lastg)
        pend = self.post_ln_begin(
            "l1a", seq, c, y, tuple(ytoks),
            wait_bychunk={cc: (ytoks[cc],) for cc in range(NC)},
            sq_bufs=[cs, sn])
        self._cs_guard = (self._cs_guard,) + tuple(pend["tst"])
        ar.pop()
        return pend

    def _l1a_head(self, seq, c, h, xn_hi, xn_lo, txn,
                  ws_q, ws_k, ws_v, ws_g, ws_o, S, y, ytoks):
        p, a, ar, pb = self.p, self.a, self.ar, self.pb
        ar.push()
        og_hi = ar.bf16(4 * 512)
        og_lo = ar.bf16(4 * 512)
        qps = {}
        kps = {}
        ws_q.nm = "l1_wq"
        self.proj_fm(ws_q, range(NC), xn_hi, xn_lo, 512, 0, 512,
                     [0, 1], [4, 5], txn, self._evac_keep(qps, [4, 5]),
                     wcol0=2 * h * 128, wfetch=256,
                     xtok_bychunk=self._l1a_xnbc)
        ws_k.nm = "l1_wk"
        self.proj_fm(ws_k, range(NC), xn_hi, xn_lo, 512, 0, 512,
                     [0, 1], [6, 7], txn, self._evac_keep(kps, [6, 7]),
                     wcol0=2 * h * 128, wfetch=256,
                     xtok_bychunk=self._l1a_xnbc)
        qp_hi = ar.bf16(1024)
        qp_lo = ar.bf16(1024)
        kp_hi = ar.bf16(1024)
        kp_lo = ar.bf16(1024)
        stbuf = [(ar.f32(512), ar.f32(512)),
                 (self._post_tmp[0], self._post_tmp[1])]
        cs, sn, tcs, tsn = self._cs
        gq = ar.f32(512)
        gk = ar.f32(512)
        ggs = self.p.site("gg")
        ggs.gate = getattr(self, "_gq_guard", None)
        ggs.dma("sync", gq, self.dram["c_gq"][h], first=True)
        tgk = ggs.dma("sync", gk, self.dram["c_gk"][h])
        tgq = tgk
        rt = {}
        prev_use = getattr(self, "_rope_prev", {})
        for (half, which) in ((0, "q"), (0, "k"), (1, "q"), (1, "k")):
            pstore = qps if which == "q" else kps
            gsc = gq if which == "q" else gk
            dh = qp_hi if which == "q" else kp_hi
            dl = qp_lo if which == "q" else kp_lo
            sA, sB = stbuf[0] if which == "q" else stbuf[1]
            x1 = pstore[0][0][:, 0:512]
            x2 = pstore[1][0][:, 0:512]
            wtok = (pstore[0][1], pstore[1][1], tcs, tsn, tgq, tgk)
            pu = prev_use.get(which)
            if pu is not None:
                wtok = wtok + (pu,)
            ca, cb = (cs, sn) if half == 0 else (cs, sn)
            a_in, b_in = (x1, x2) if half == 0 else (x2, x1)
            t1 = self.v("tensor_mul", sA, a_in, ca, wait=wtok, inc=True)
            t2 = self.v("tensor_mul", sB, b_in, cb, wait=wtok, inc=True)
            op = "tensor_sub" if half == 0 else "tensor_add"
            t3 = self.v(op, sA, sA, sB, wait=(t1, t2), inc=True)
            t4 = self.v("tensor_mul", sA, sA, gsc, wait=(t3,), inc=True)
            t5 = self.split(sA, dh[:, half * 512:(half + 1) * 512],
                            dl[:, half * 512:(half + 1) * 512], wait=(t4,))
            rt[(which, half)] = t5
            prev_use[which] = t5
        self._rope_prev = prev_use
        for pstore in (qps, kps):
            self.guard(pstore[0][2], rt[("q", 1)] if pstore is qps else rt[("k", 1)])
            self.guard(pstore[1][2], rt[("q", 1)] if pstore is qps else rt[("k", 1)])
        rtoks = [rt[("q", 1)], rt[("k", 1)]]
        rope_kdc = {0: (rt[("q", 0)], rt[("k", 0)]),
                    1: (rt[("q", 1)], rt[("k", 1)])}
        self._gq_guard = rtoks[-1]
        self._cs_guard = rtoks[-1]
        ktm_hi = ar.bf16(1024)
        ktm_lo = ar.bf16(1024)
        kts = self.p.site("ktm")
        firstt = True
        for kdc in range(2):
            for tch in range(4):
                for (s_, d_) in ((kp_hi, ktm_hi), (kp_lo, ktm_lo)):
                    kts.dma(
                        "scalar",
                        d_[:, tch * 256 + kdc * 128: tch * 256 + (kdc + 1) * 128],
                        s_[:, kdc * 512 + tch * 128: kdc * 512 + (tch + 1) * 128],
                        wait=(rtoks[1],), first=firstt, transpose=True)
                    firstt = False
        ttoks = [kts.tok()]
        vtm_hi = ar.bf16(4 * 512)
        vtm_lo = ar.bf16(4 * 512)
        stg2 = self._post_tmp
        self._brep_tok = ()
        def evac_v(j, tch, ps, tok):
            th = self.s("copy", vtm_hi[:, j * 512:(j + 1) * 512], ps[:, 0:512],
                        wait=tok, inc=True)
            return self.v("tensor_sub", vtm_lo[:, j * 512:(j + 1) * 512],
                          ps[:, 0:512], vtm_hi[:, j * 512:(j + 1) * 512],
                          wait=(th,), inc=True)
        tv = self.proj_tm(ws_v, range(NC), xn_hi, xn_lo, 512, range(4),
                          h * 512, (h + 1) * 512, [0, 1, 2, 3], txn, evac_v,
                          xtok_bychunk=self._l1a_xnbc)
        wt_hi = ar.bf16(4 * 512)
        wt_lo = ar.bf16(4 * 512)
        est = gq
        stoks = []
        npass = len(self.pairs(0, 0, 0, 0))
        sprev = None
        for sb in range(4):
            bank = 4 + sb
            gtok = None
            i = 0
            for kdc in range(2):
                for ip, (A, Bb) in enumerate(self.pairs(
                        kp_hi[:, kdc * 512 + sb * 128: kdc * 512 + (sb + 1) * 128],
                        kp_lo[:, kdc * 512 + sb * 128: kdc * 512 + (sb + 1) * 128],
                        qp_hi[:, kdc * 512:(kdc + 1) * 512],
                        qp_lo[:, kdc * 512:(kdc + 1) * 512])):
                    first = i == 0
                    last = i == 2 * npass - 1
                    gtok = self.mm(bank if first else None, pb[bank][:, :],
                                   A, Bb, start=first, stop=last,
                                   wait=rope_kdc[kdc], inc=last)
                    i += 1
            tm = self.v("tensor_mul", est, pb[bank][:, :],
                        a["l1mask"][:, sb * 512:(sb + 1) * 512],
                        wait=(gtok,) + self.t_const +
                        ((sprev,) if sprev else ()), inc=True)
            self.guard(bank, tm)
            sprev = self.split(est, wt_hi[:, sb * 512:(sb + 1) * 512],
                               wt_lo[:, sb * 512:(sb + 1) * 512], wait=(tm,))
            stoks.append(sprev)
        self._gq_guard = stoks[-1]
        obanks = [6, 7, 2, 3]
        o32 = ar.f32(4 * 512)
        Sh = S[:, h * 1024:(h + 1) * 1024]
        Ssp_hi = ar.bf16(1024)
        Ssp_lo = ar.bf16(1024)
        tS = None
        if c > 0:
            tS = self.split(Sh, Ssp_hi, Ssp_lo, wait=(self._S_tok,))
        otoks = []
        for vb in range(4):
            bank = obanks[vb]
            gtok = None
            i = 0
            seqs = []
            if c > 0:
                for kdc in range(2):
                    seqs += self.pairs(
                        Ssp_hi[:, kdc * 512 + vb * 128: kdc * 512 + (vb + 1) * 128],
                        Ssp_lo[:, kdc * 512 + vb * 128: kdc * 512 + (vb + 1) * 128],
                        qp_hi[:, kdc * 512:(kdc + 1) * 512],
                        qp_lo[:, kdc * 512:(kdc + 1) * 512])
            for sb in range(4):
                seqs += self.pairs(
                    vtm_hi[:, sb * 512 + vb * 128: sb * 512 + (vb + 1) * 128],
                    vtm_lo[:, sb * 512 + vb * 128: sb * 512 + (vb + 1) * 128],
                    wt_hi[:, sb * 512:(sb + 1) * 512],
                    wt_lo[:, sb * 512:(sb + 1) * 512])
            for i, (A, Bb) in enumerate(seqs):
                first = i == 0
                last = i == len(seqs) - 1
                gtok = self.mm(bank if first else None, pb[bank][:, :], A, Bb,
                               start=first, stop=last,
                               wait=((tS,) if tS else ()) + tuple(stoks) +
                               tuple(tv) + (rtoks[0],), inc=last)
            te = self.s("copy", o32[:, vb * 512:(vb + 1) * 512], pb[bank][:, :],
                        wait=(gtok,), inc=True)
            self.guard(bank, te)
            otoks.append(te)
        gC = GAMMAS[h] ** CCH
        dst_stage = stg2
        sutoks = []
        for kdc in range(2):
            bank = kdc
            gtok = None
            seqs = []
            for sb in range(4):
                seqs += self.pairs(
                    ktm_hi[:, sb * 256 + kdc * 128: sb * 256 + (kdc + 1) * 128],
                    ktm_lo[:, sb * 256 + kdc * 128: sb * 256 + (kdc + 1) * 128],
                    vtm_hi[:, sb * 512:(sb + 1) * 512],
                    vtm_lo[:, sb * 512:(sb + 1) * 512])
            for i, (A, Bb) in enumerate(seqs):
                first = i == 0
                last = i == len(seqs) - 1
                gtok = self.mm(bank if first else None, pb[bank][:, :], A, Bb,
                               start=first, stop=last,
                               wait=tuple(ttoks) + tuple(tv), inc=last)
            tsc = self.s("activation", dst_stage[kdc][:, :], pb[bank][:, :],
                         AF.Identity, scale=gC, wait=(gtok,), inc=True)
            self.guard(bank, tsc)
            tup = self.v("scalar_tensor_tensor",
                         Sh[:, kdc * 512:(kdc + 1) * 512],
                         Sh[:, kdc * 512:(kdc + 1) * 512], float(gC),
                         dst_stage[kdc][:, :], ALU.mult, ALU.add,
                         wait=(tsc,) + (((tS,) if tS else ())), inc=True)
            sutoks.append(tup)
        self._S_tok = sutoks[-1]
        self.p.site("ktm").gate = sutoks[-1]
        s2o = ar.row(512)
        sq = stg2[0]
        t2 = None
        sqprev = None
        for vb in range(4):
            ts = self.s("activation", sq[:, :], o32[:, vb * 512:(vb + 1) * 512],
                        AF.Square, wait=tuple(otoks) + tuple(sutoks) +
                        ((sqprev,) if sqprev else ()), inc=True)
            t2 = self.mm(4 if vb == 0 else None, pb[4][0:1, :],
                         a["ones_col"], sq[:, :], start=(vb == 0),
                         stop=(vb == 3), wait=(ts,), inc=True)
            sqprev = t2
        trow = self.s("activation", s2o, pb[4][0:1, :], AF.Sqrt,
                      scale=1.0 / DV1, bias=self.a["eps"][0:1, :],
                      wait=(t2, self.t_eps), inc=True)
        self.guard(4, trow)
        tinv = self.v("reciprocal", s2o, s2o, wait=(trow,), inc=True)
        rep, trep = self.replicate(s2o, 512, (5,), (tinv,))
        gps = {}
        gtoks = self.proj_fm(ws_g, range(NC), xn_hi, xn_lo, 512, 0, 512,
                             range(4), [0, 1, 6, 7], txn,
                             self._evac_keep(gps, [0, 1, 6, 7]), wcol0=h * 512,
                             xtok_bychunk=self._l1a_xnbc)
        self._l1a_lastg = tuple(gtoks)
        res = []
        for vb in range(4):
            to1 = self.v("scalar_tensor_tensor", sq[:, :],
                         o32[:, vb * 512:(vb + 1) * 512],
                         self.bias_col("l1_gnw", vb), rep[0][0][:, 0:512],
                         ALU.mult, ALU.mult,
                         wait=(trep[0],) + tuple(otoks) + self.t_bias +
                         ((res[-1],) if res else ()), inc=True)
            to2 = self.v("tensor_mul", sq[:, :], sq[:, :],
                         gps[vb][0][:, 0:512], wait=(to1,) + (gps[vb][1],),
                         inc=True)
            tsp = self.split(sq, og_hi[:, vb * 512:(vb + 1) * 512],
                             og_lo[:, vb * 512:(vb + 1) * 512], wait=(to2,))
            self.guard(gps[vb][2], tsp)
            res.append(tsp)
        self.guard(5, res[-1])
        ws_o.nm = "l1_wo"
        npass = len(self.pairs(0, 0, 0, 0))
        newytoks = []
        gt = {}
        for vb in range(4):
            kc = h * 4 + vb
            slot = vb % ws_o.nslots
            tf = ws_o.fetch(slot, kc)
            ulast = None
            for oc in range(NC):
                bank = oc
                whi, wlo = ws_o.blk(slot, oc * 128, 128)
                for ip, (A, Bb) in enumerate(self.pairs(
                        whi, wlo,
                        og_hi[:, vb * 512:(vb + 1) * 512],
                        og_lo[:, vb * 512:(vb + 1) * 512])):
                    gt[bank] = self.mm(bank if vb == 0 and ip == 0 else None,
                                       pb[bank][:, :], A, Bb,
                                       start=(vb == 0 and ip == 0),
                                       stop=(vb == 3 and ip == npass - 1),
                                       wait=tf + (res[vb],),
                                       inc=(ip == npass - 1))
                ulast = gt[bank]
            ws_o.use(slot, ulast)
        for oc in range(NC):
            bank = oc
            if ytoks[oc] is None:
                ty = self.s("copy", y[:, oc * 512:(oc + 1) * 512],
                            pb[bank][:, :], wait=(gt[bank],), inc=True)
            else:
                ty = self.v("tensor_tensor", y[:, oc * 512:(oc + 1) * 512],
                            y[:, oc * 512:(oc + 1) * 512], pb[bank][:, :],
                            ALU.add, wait=(gt[bank], ytoks[oc]), inc=True)
            self.guard(bank, ty)
            newytoks.append(ty)
        ar.pop()
        return newytoks

    def _evac_keep(self, store, banks):
        def evac(j, oc, ps, tok):
            store[j] = (ps, tok[0], banks[j])
            return tok[0]
        return evac

    def l1f(self):
        ar = self.ar
        ar.push()
        self._ws_ctr = 0
        ws_q = WStream(self, "f_qwt", 128, nslots=4)
        ws_k = WStream(self, "f_kwt", 128, nslots=4)
        ws_v = WStream(self, "f_vwt", 128, nslots=4)
        ws_o = WStream(self, "f_owt", D)
        self._l1f_y = [ar.f32(NC * 512), ar.f32(NC * 512)]
        self._l1f_psq = [ar.f32(512), ar.f32(512)]
        self._l1f_y_war = {}
        self._l1f_octr = 0
        tb = self.p.site("brl1f").dma(
            "sync", self.a["brep"][:, :], self.dram["f_vb_rep"],
            wait=tuple(getattr(self, "xs_barrier", ())))
        self._brep_tok = (tb,)
        pend = None
        for seq in range(SEQ):
            pend = self._l1f_seq(seq, ws_q, ws_k, ws_v, ws_o, pend)
        self._l1f_finish(pend)
        ar.pop()

    def _l1f_finish(self, pend):
        release, _ = self.post_ln_finish(pend)
        self._l1f_y_war[pend["ypar"]] = (tuple(self._last_post_t3) +
                                         tuple(pend["tst"]))

    def _l1f_seq(self, seq, ws_q, ws_k, ws_v, ws_o, pend):
        p, a, ar, pb = self.p, self.a, self.ar, self.pb
        src = self.x_src("l1f")
        ar.push()
        off = [0]
        for e in range(NEXP):
            off.append(off[-1] + MCH[e])
        xm_hi = ar.bf16(off[-1] * 512)
        xm_lo = ar.bf16(off[-1] * 512)
        rows = []
        xmtoks = []
        for e in range(NEXP):
            arow_keep = ar.row(512)
            brow_keep = ar.row(512)
            rows.append((arow_keep, brow_keep))
        for e in range(NEXP):
            ar.push()
            xwin = ar.f32(NC * 512)
            txn = self.ln_to_splits(
                src, seq, e * 512, 512, xwin,
                xm_hi[:, off[e] * 512:], xm_lo[:, off[e] * 512:],
                affine=(lambda cc: self.bias_col("f_nw", cc),
                        lambda cc: self.bias_col("f_nb", cc)),
                row_keep=rows[e], chunks=range(MCH[e]))
            xmtoks += list(txn)
            self._l1f_xmbc = getattr(self, "_l1f_xmbc", {})
            self._l1f_xmbc[e] = dict(self.xn_bychunk)
            ar.pop()
        on_hi = ar.bf16(off[-1] * 512)
        on_lo = ar.bf16(off[-1] * 512)
        ontoks = []
        for h in range(H2):
            taus = [tau for tau in range(NEXP) if h < MCH[tau]]
            if not taus:
                continue
            ontoks += self._l1f_head(seq, h, taus, ws_q, ws_k, ws_v,
                                     xm_hi, xm_lo, xmtoks, off, on_hi, on_lo)
        for tau in range(NEXP):
            pend = self._l1f_out(seq, tau, ws_o, on_hi, on_lo, ontoks, off,
                                 rows, pend)
        ar.pop()
        return pend

    def _l1f_head(self, seq, h, taus, ws_q, ws_k, ws_v, xm_hi, xm_lo, xmtoks,
                  off, on_hi, on_lo):
        a, ar, pb = self.a, self.ar, self.pb
        ar.push()
        zrows = {}
        Oacc = {}
        for tau in taus:
            zrows[(h, tau)] = ar.row(512)
            Oacc[(h, tau)] = ar.f32(512)
        qh_hi = ar.bf16(len(taus) * 512)
        qh_lo = ar.bf16(len(taus) * 512)
        kh_hi = ar.bf16(512)
        kh_lo = ar.bf16(512)
        vh_hi = ar.bf16(4 * 128)
        vh_lo = ar.bf16(4 * 128)
        wt_hi = ar.bf16(4 * 512)
        wt_lo = ar.bf16(4 * 512)
        est = ar.f32(512)
        stg2 = [ar.f32(512), ar.f32(128)]
        npass = len(self.pairs(0, 0, 0, 0))
        qtoks = []
        for j, tau in enumerate(taus):
            qh = qh_hi[:, j * 512:]
            qlo = qh_lo[:, j * 512:]
            qtoks += self.proj_fm(
                ws_q, range(MCH[tau]),
                xm_hi[:, off[tau] * 512:], xm_lo[:, off[tau] * 512:],
                512, 0, 512, [0], [0], tuple(xmtoks),
                self.evac_split_bias(qh, qlo, 512, 512, "f_qb",
                                     bias_idx=lambda oc, hh=h: hh),
                wcol0=h * 128, xtok_bychunk=self._l1f_xmbc[tau])
        first_e = True
        ztoks = []
        otoks = []
        for e in range(NEXP):
            ktoks = self.proj_fm(
                ws_k, range(MCH[e]),
                xm_hi[:, off[e] * 512:], xm_lo[:, off[e] * 512:],
                512, 0, 512, [0], [1], tuple(xmtoks),
                self.evac_split_bias(kh_hi, kh_lo, 512, 512, "f_kb",
                                     bias_idx=lambda oc, hh=h: hh),
                wcol0=h * 128, xtok_bychunk=self._l1f_xmbc[e])
            vtoks = self.proj_tm(
                ws_v, range(MCH[e]),
                xm_hi[:, off[e] * 512:], xm_lo[:, off[e] * 512:],
                512, range(4), h * 128, (h + 1) * 128, [2, 3, 6, 7],
                tuple(xmtoks),
                self.evac_split_brep(vh_hi, vh_lo, 128, 128,
                                     a["brep"][:, h * 128:(h + 1) * 128],
                                     stg2),
                xtok_bychunk=self._l1f_xmbc[e])
            for j, tau in enumerate(taus):
                sprev = None
                zt = None
                for sb in range(4):
                    bank = 4 + sb
                    gtok = None
                    for ip, (A, Bb) in enumerate(self.pairs(
                            kh_hi[:, sb * 128:(sb + 1) * 128],
                            kh_lo[:, sb * 128:(sb + 1) * 128],
                            qh_hi[:, j * 512:(j + 1) * 512],
                            qh_lo[:, j * 512:(j + 1) * 512])):
                        first = ip == 0
                        last = ip == npass - 1
                        gtok = self.mm(bank if first else None, pb[bank][:, :],
                                       A, Bb, start=first, stop=last,
                                       wait=tuple(ktoks) + tuple(qtoks),
                                       inc=last)
                    te = self.s("activation", est, pb[bank][:, :], AF.Exp,
                                scale=1.0 / 32.0,
                                wait=(gtok,) + ((sprev,) if sprev else ()) +
                                ((zt,) if zt is not None else ()),
                                inc=True)
                    self.guard(bank, te)
                    zt = self.mm(0 if sb == 0 else None, pb[0][0:1, :],
                                 a["ones_col"], est, start=(sb == 0),
                                 stop=(sb == 3), wait=(te,), inc=True)
                    sprev = self.split(est, wt_hi[:, sb * 512:(sb + 1) * 512],
                                       wt_lo[:, sb * 512:(sb + 1) * 512],
                                       wait=(te,))
                if first_e:
                    tz = self.s("copy", zrows[(h, tau)], pb[0][0:1, :],
                                wait=(zt,), inc=True)
                else:
                    tz = self.v("tensor_tensor", zrows[(h, tau)],
                                zrows[(h, tau)], pb[0][0:1, :], ALU.add,
                                wait=(zt,) + tuple(ztoks[-1:]), inc=True)
                self.guard(0, tz)
                ztoks.append(tz)
                bank = 1
                gtok = None
                seqs = []
                for sb in range(4):
                    seqs += self.pairs(
                        vh_hi[:, sb * 128:(sb + 1) * 128],
                        vh_lo[:, sb * 128:(sb + 1) * 128],
                        wt_hi[:, sb * 512:(sb + 1) * 512],
                        wt_lo[:, sb * 512:(sb + 1) * 512])
                for i, (A, Bb) in enumerate(seqs):
                    gtok = self.mm(bank if i == 0 else None, pb[bank][:, :],
                                   A, Bb, start=(i == 0),
                                   stop=(i == len(seqs) - 1),
                                   wait=(sprev,) + tuple(vtoks),
                                   inc=(i == len(seqs) - 1))
                if first_e:
                    to = self.s("copy", Oacc[(h, tau)], pb[bank][:, :],
                                wait=(gtok,), inc=True)
                else:
                    to = self.v("tensor_tensor", Oacc[(h, tau)],
                                Oacc[(h, tau)], pb[bank][:, :], ALU.add,
                                wait=(gtok,) + tuple(otoks[-1:]), inc=True)
                self.guard(bank, to)
                otoks.append(to)
            first_e = False
        res = []
        for j, tau in enumerate(taus):
            rz = ar.row(512)
            tr = self.v("reciprocal", rz, zrows[(h, tau)],
                        wait=tuple(ztoks), inc=True)
            rep, trep = self.replicate(rz, 512, (0,), (tr,))
            tn = self.v("tensor_mul", est, Oacc[(h, tau)], rep[0][0][:, 0:512],
                        wait=(trep[0],) + tuple(otoks) +
                        ((res[-1],) if res else ()), inc=True)
            self.guard(0, tn)
            tsp = self.split(est, on_hi[:, (off[tau] + h) * 512:
                                        (off[tau] + h + 1) * 512],
                             on_lo[:, (off[tau] + h) * 512:
                                   (off[tau] + h + 1) * 512], wait=(tn,))
            res.append(tsp)
        ar.pop()
        return res

    def _l1f_out(self, seq, tau, ws_o, on_hi, on_lo, ontoks, off, rows, pend):
        a, ar, pb = self.a, self.ar, self.pb
        src = self.x_src("l1f")
        ar.push()
        xt = ar.f32(NC * 512)
        xn4 = ar.f32(NC * 512)
        par = self._l1f_octr % 2
        self._l1f_octr += 1
        y = self._l1f_y[par]
        ywar = tuple(self._l1f_y_war.get(par, ()))
        tx = self.load_x(src, seq, tau * 512, 512, xt)
        arow, brow = rows[tau]
        a_ps, _ = self.replicate(arow, 512, (4,), tuple(ontoks[-1:]))
        b_ps, _ = self.replicate(brow, 512, (5,), tuple(ontoks[-1:]))
        ar.push()
        prev = [None, None]
        xntoks = []
        for c in range(NC):
            i = c % 2
            t1 = self.v("tensor_tensor", xn4[:, c * 512:(c + 1) * 512],
                        xt[:, c * 512:(c + 1) * 512], a_ps[0][0][:, 0:512],
                        ALU.mult, wait=tuple(tx) + (a_ps[0][3], b_ps[0][3]) +
                        ((prev[i],) if prev[i] else ()), inc=True)
            t2 = self.v("tensor_sub", xn4[:, c * 512:(c + 1) * 512],
                        xn4[:, c * 512:(c + 1) * 512], b_ps[0][0][:, 0:512],
                        wait=(t1,), inc=True)
            t3 = self.v("scalar_tensor_tensor", xn4[:, c * 512:(c + 1) * 512],
                        xn4[:, c * 512:(c + 1) * 512],
                        self.bias_col("f_nw", c),
                        self.bias_col("f_nb", c).broadcast_to((128, 512)),
                        ALU.mult, ALU.add, wait=(t2,), inc=True)
            prev[i] = t3
            xntoks.append(t3)
        ar.pop()
        self.guard(4, xntoks[-1])
        self.guard(5, xntoks[-1])
        if pend is not None:
            self._l1f_finish(pend)
        m = MCH[tau]
        npass = len(self.pairs(0, 0, 0, 0))
        gt = {}
        for kc in range(m):
            slot = kc % 2
            tf = ws_o.fetch(slot, kc)
            ulast = None
            for oc in range(m):
                whi, wlo = ws_o.blk(slot, oc * 128, 128)
                for ip, (A, Bb) in enumerate(self.pairs(
                        whi, wlo,
                        on_hi[:, (off[tau] + kc) * 512:(off[tau] + kc + 1) * 512],
                        on_lo[:, (off[tau] + kc) * 512:(off[tau] + kc + 1) * 512])):
                    gt[oc] = self.mm(oc if kc == 0 and ip == 0 else None,
                                     pb[oc][:, :], A, Bb,
                                     start=(kc == 0 and ip == 0),
                                     stop=(kc == m - 1 and ip == npass - 1),
                                     wait=tf + tuple(ontoks),
                                     inc=(ip == npass - 1))
                ulast = gt[oc]
            ws_o.use(slot, ulast)
        ytoks = []
        for oc in range(NC):
            if oc >= m:
                ytoks.append(self.v("tensor_copy",
                                    y[:, oc * 512:(oc + 1) * 512],
                                    xn4[:, oc * 512:(oc + 1) * 512],
                                    wait=(xntoks[oc],) + ywar, inc=True))
                continue
            t1 = self.s("activation", y[:, oc * 512:(oc + 1) * 512],
                        pb[oc][:, :], AF.Identity,
                        bias=self.bias_col("f_ob", oc),
                        wait=(gt[oc],) + self.t_bias + ywar, inc=True)
            self.guard(oc, t1)
            t2 = self.v("tensor_add", y[:, oc * 512:(oc + 1) * 512],
                        y[:, oc * 512:(oc + 1) * 512],
                        xn4[:, oc * 512:(oc + 1) * 512],
                        wait=(t1, xntoks[oc]), inc=True)
            ytoks.append(t2)
        pend = self.post_ln_begin(
            "l1f", seq, tau, y, tuple(ytoks),
            wait_bychunk={oc: (ytoks[oc],) for oc in range(NC)},
            sq_bufs=self._l1f_psq)
        pend["ypar"] = par
        ar.pop()
        return pend

    def finish(self):
        if self.out_toks:
            last = self.out_toks[-1]
            self.p.items["gpsimd"].append((None, self.p._norm("gpsimd", (last,)),
                                           None, False, False))

    def build(self):
        self.cur_blk = "p0"
        self.phase0()
        self.xs_barrier = self._wsp_done
        for blk in self.blocks:
            self.cur_blk = blk
            getattr(self, blk)()
            self.xs_barrier = (self.p.site("ps0" + blk).tok(),
                               self.p.site("ps1" + blk).tok()) + \
                tuple(getattr(self, "_g12_done", ()))
        self.finish()
        self.p.replay()



_CACHE = {}


def _build(passes, blocks):
    key = (passes, tuple(blocks))
    if key not in _CACHE:
        nc = bass.Bass("TRN2", target_bir_lowering=False)
        b = Builder(nc, passes=passes, blocks=blocks)
        b.build()
        _CACHE[key] = (nc, b)
    return _CACHE[key]


def make_inputs(inputs, core):
    c = _host_constants()
    x = np.asarray(inputs["L0_attn_x"])
    xs = x[core * SEQ:(core + 1) * SEQ]
    xin = np.ascontiguousarray(xs.transpose(0, 2, 1)).reshape(SEQ, NC, 128, T)
    L_r, R_r = _host_tt_factors(inputs["L0_ffn_G1_R"], inputs["L0_ffn_G2_R"])
    L_i, R_i = _host_tt_factors(inputs["L0_ffn_G1_I"], inputs["L0_ffn_G2_I"])
    m = {
        "xin": xin,
        "l0_wq": np.asarray(inputs["L0_attn_Wq"]),
        "l0_wk": np.asarray(inputs["L0_attn_Wk"]),
        "l0_wv": np.asarray(inputs["L0_attn_Wv"]),
        "l0_bq": _fm(inputs["L0_attn_bq"]),
        "l0_bk": _fm(inputs["L0_attn_bk"]),
        "l0_bv_rep": np.ascontiguousarray(
            np.broadcast_to(np.asarray(inputs["L0_attn_bv"], np.float32),
                            (128, D))),
        "tt_lr": L_r, "tt_rr": R_r, "tt_li": L_i, "tt_ri": R_i,
        "l0f_br": _fm(inputs["L0_ffn_bias_real"]),
        "l0f_bi": _fm(inputs["L0_ffn_bias_imag"]),
        "l1_wq": np.asarray(inputs["L1_attn_Wq"]),
        "l1_wk": np.asarray(inputs["L1_attn_Wk"]),
        "l1_wv": np.asarray(inputs["L1_attn_Wv"]),
        "l1_wg": np.asarray(inputs["L1_attn_Wg"]),
        "l1_wo": np.asarray(inputs["L1_attn_Wo"]),
        "l1_gnw": _fm(inputs["L1_attn_gn_weight"]),
        "f_qwt": np.ascontiguousarray(np.asarray(inputs["L1_ffn_q_w"]).T),
        "f_kwt": np.ascontiguousarray(np.asarray(inputs["L1_ffn_k_w"]).T),
        "f_vwt": np.ascontiguousarray(np.asarray(inputs["L1_ffn_v_w"]).T),
        "f_owt": np.ascontiguousarray(np.asarray(inputs["L1_ffn_o_w"]).T),
        "f_nw": _fm(inputs["L1_ffn_norm_w"]),
        "f_nb": _fm(inputs["L1_ffn_norm_b"]),
        "f_qb": _fm(inputs["L1_ffn_q_b"]),
        "f_kb": _fm(inputs["L1_ffn_k_b"]),
        "f_vb_rep": np.ascontiguousarray(
            np.broadcast_to(np.asarray(inputs["L1_ffn_v_b"], np.float32),
                            (128, D))),
        "f_ob": _fm(inputs["L1_ffn_o_b"]),
    }
    m.update(c)
    return {k: np.ascontiguousarray(v) for k, v in m.items()}


def kernel(**inputs):
    nc, b = _build(3, ("l0a", "l0f", "l1a", "l1f"))
    in_maps = [make_inputs(inputs, core) for core in range(NCORES)]
    res = run_bass_kernel_spmd(nc, in_maps, core_ids=list(range(NCORES)))
    outs = []
    for core in range(NCORES):
        o = np.asarray(res.results[core]["out"])
        outs.append(o.reshape(SEQ, D, T).transpose(0, 2, 1))
    return np.ascontiguousarray(np.concatenate(outs, axis=0), dtype=np.float32)
